# version 19
# baseline (speedup 1.0000x reference)
import numpy as np
import concourse.bass as bass
import concourse.mybir as mybir
from concourse.bass_utils import run_bass_kernel_spmd
from concourse.alu_op_type import AluOpType

F32 = mybir.dt.float32
BF16 = mybir.dt.bfloat16
I32 = mybir.dt.int32
AF = mybir.ActivationFunctionType
AXX = mybir.AxisListType.X

D = 1024
S = 4096
OT = 2048
NQ = 4
ENGINES = ("sync", "gpsimd", "tensor", "vector", "scalar")
DEBUG = False
DEBUG2 = False
SER = set()


def _blocks(par):
    bs = []
    for j in range(16):
        if j % 2 == par:
            bs.append(j)
            bs.append(31 - j)
    return sorted(bs)


B0 = _blocks(0)
B1 = _blocks(1)
MAXG = [max(B0[r], B1[r]) for r in range(16)]
MING = [min(B0[r], B1[r]) for r in range(16)]
KCQ = [MAXG[4 * q + 3] + 1 for q in range(NQ)]
KGQ = [kc // 2 for kc in KCQ]


def _suffix(q, kc):
    for r in range(4):
        if MAXG[4 * q + r] >= kc:
            return r
    raise AssertionError


MASKS = []
MIDX = {}
for _q in range(NQ):
    for _r in range(4):
        for _kc in range(MING[4 * _q + _r], MAXG[4 * _q + _r] + 1):
            MIDX[(_q, _kc, _r)] = len(MASKS)
            MASKS.append((_q, _kc, _r))
NM = len(MASKS)
assert NM == 32, NM

ST_LN = 4
ST_WQ = 30
ST_CTG = 35
ST_CTU = 39
ST_C2 = 43
ST_HQ = 47
ST_YQ = 51


def build_nc():
    nc = bass.Bass("TRN2", target_bir_lowering=False, detect_race_conditions=False)

    x_full = nc.dram_tensor("x_full", [S, D], F32, kind="ExternalInput").ap()
    x_own = nc.dram_tensor("x_own", [OT, D], F32, kind="ExternalInput").ap()
    deltas = nc.dram_tensor("deltas", [128, NM], F32, kind="ExternalInput").ap()
    wq_in = nc.dram_tensor("wq_in", [D, D], F32, kind="ExternalInput").ap()
    wk_in = nc.dram_tensor("wk_in", [D, D], F32, kind="ExternalInput").ap()
    wv_in = nc.dram_tensor("wv_in", [D, D], F32, kind="ExternalInput").ap()
    wo_in = nc.dram_tensor("wo_in", [D, D], F32, kind="ExternalInput").ap()
    wgT_in = nc.dram_tensor("wgT_in", [D, 4096], F32, kind="ExternalInput").ap()
    wuT_in = nc.dram_tensor("wuT_in", [D, 4096], F32, kind="ExternalInput").ap()
    wdT_in = nc.dram_tensor("wdT_in", [4096, D], F32, kind="ExternalInput").ap()
    y_own = nc.dram_tensor("y_own", [OT, D], F32, kind="ExternalOutput").ap()

    y1T_s = nc.dram_tensor("y1T_s", [D, S], F32, kind="Internal").ap()
    y1To_s = nc.dram_tensor("y1To_s", [D, OT], F32, kind="Internal").ap()
    KT_s = nc.dram_tensor("KT_s", [D, S], F32, kind="Internal").ap()
    V_s = nc.dram_tensor("V_s", [32, 128, 1040], F32, kind="Internal").ap()
    QT_s = nc.dram_tensor("QT_s", [D, OT], F32, kind="Internal").ap()
    xmid_s = nc.dram_tensor("xmid_s", [OT, D], F32, kind="Internal").ap()
    wgq_s = nc.dram_tensor("wgq_s", [D, 4096], BF16, kind="Internal").ap()
    if DEBUG2:
        dbg_xmid2 = nc.dram_tensor("dbg_xmid2", [OT, D], F32, kind="ExternalOutput").ap()
        dbg_wgq = nc.dram_tensor("dbg_wgq", [D, 4096], BF16, kind="ExternalOutput").ap()
        dbg_wuq = nc.dram_tensor("dbg_wuq", [D, 4096], BF16, kind="ExternalOutput").ap()
        dbg_wdq = nc.dram_tensor("dbg_wdq", [4096, D], BF16, kind="ExternalOutput").ap()
    if DEBUG:
        dbg_y2qT = nc.dram_tensor("dbg_y2qT", [128, 4096], BF16, kind="ExternalOutput").ap()
        dbg_gh = nc.dram_tensor("dbg_gh", [128, 4096], F32, kind="ExternalOutput").ap()
        dbg_hqT = nc.dram_tensor("dbg_hqT", [128, 4096], BF16, kind="ExternalOutput").ap()
        dbg_stat = nc.dram_tensor("dbg_stat", [128, 64], F32, kind="ExternalOutput").ap()
        dbg_xmid = nc.dram_tensor("dbg_xmid", [OT, D], F32, kind="ExternalOutput").ap()
        dbg_y1T = nc.dram_tensor("dbg_y1T", [D, S], F32, kind="ExternalOutput").ap()
        dbg_KT = nc.dram_tensor("dbg_KT", [D, 512], F32, kind="ExternalOutput").ap()
        dbg_QT = nc.dram_tensor("dbg_QT", [D, 512], F32, kind="ExternalOutput").ap()
        dbg_P = nc.dram_tensor("dbg_P", [128, 1024], F32, kind="ExternalOutput").ap()
    wuq_s = nc.dram_tensor("wuq_s", [D, 4096], BF16, kind="Internal").ap()
    wdq_s = nc.dram_tensor("wdq_s", [4096, D], BF16, kind="Internal").ap()

    ARF = 30000
    ARB = 31000
    arf = nc.alloc_sbuf_tensor("arf", [128, ARF], F32)
    arb = nc.alloc_sbuf_tensor("arb", [128, ARB], BF16)
    ident = nc.alloc_sbuf_tensor("ident", [128, 128], F32)
    iof = nc.alloc_sbuf_tensor("iof", [128, 128], F32)
    iop = nc.alloc_sbuf_tensor("iop", [128, 1], F32)
    shiftm = nc.alloc_sbuf_tensor("shiftm", [64, 128], F32)
    onesrow = nc.alloc_sbuf_tensor("onesrow", [1, 128], F32)
    onescol = nc.alloc_sbuf_tensor("onescol", [128, 1], F32)
    bdg = nc.alloc_sbuf_tensor("bdg", [128, 2], F32)
    negmb = nc.alloc_sbuf_tensor("negmb", [128, 1], F32)
    mw127 = nc.alloc_sbuf_tensor("mw127", [128, 3], F32)
    dlt = nc.alloc_sbuf_tensor("dlt", [128, NM], F32)
    i32t = nc.alloc_sbuf_tensor("i32t", [128, 2048], I32)
    i32y = nc.alloc_sbuf_tensor("i32y", [128, 1024], I32)
    stat = nc.alloc_sbuf_tensor("stat", [128, 56], F32)
    ss = nc.alloc_sbuf_tensor("ss", [2, 104], F32)
    sm = nc.alloc_sbuf_tensor("sm", [1, 24], F32)
    identb = nc.alloc_sbuf_tensor("identb", [128, 128], BF16)
    epsb = nc.alloc_sbuf_tensor("epsb", [128, 1], F32)

    psA = nc.alloc_psum_tensor("psA", [128, 1024], F32)
    psA2 = nc.alloc_psum_tensor("psA2", [128, 1024], F32)
    psB = nc.alloc_psum_tensor("psB", [128, 1024], F32)
    psC = nc.alloc_psum_tensor("psC", [128, 1024], F32)

    LSEMS = ("Lxb0", "Lxb1", "Lqw0", "Lqw1", "Lyb0", "Lyb1", "Lwb", "Lkv0",
             "Lkv1", "Lqt", "Lwo", "Lxo0", "Lxo1", "Lxl0", "Lxl1", "Lwg0",
             "Lwg1", "Lwd0", "Lwd1", "Ldlt", "Lxr0", "Lxr1")
    SSEMS = ("Syst0", "Syst1", "Svst0", "Svst1", "Sqst0", "Sqst1", "Swqt0",
             "Swqt1", "Sxm0", "Sxm1", "Sxout", "Sy", "st")
    SEM = {}
    for n in ("ld", "te", "ve", "se", "ge", "sy") + LSEMS + SSEMS:
        SEM[n] = nc.alloc_semaphore(name=n)
    CNT = dict.fromkeys(SEM, 0)

    queues = {e: [] for e in ENGINES}

    def step(eng, waits, emitfn, incsem=None, amt=1):
        mark = None
        if incsem is not None:
            CNT[incsem] += amt
            mark = CNT[incsem]
        wl = [(s, v) for (s, v) in waits if v and v > 0]

        def fn(e, wl=wl, emitfn=emitfn, incsem=incsem, amt=amt):
            for s, v in wl:
                e.wait_ge(SEM[s], v)
            ins = emitfn(e)
            if incsem is not None and ins is not None:
                ins.then_inc(SEM[incsem], amt)
        queues[eng].append(fn)
        return mark

    def dld(e, sem, out, in_):
        e.dma_start(out=out, in_=in_).then_inc(SEM[sem], 16)

    def barrier():
        lwl = [(n, CNT[n]) for n in LSEMS + ("ld",)] + [("sy", CNT["sy"])]
        swl = [(n, CNT[n]) for n in SSEMS] + [("ge", CNT["ge"])]
        step("sync", lwl, lambda e: e.nop(), "sy")
        step("gpsimd", swl, lambda e: e.nop(), "ge")
        step("tensor", [("te", CNT["te"])], lambda e: e.nop(), "te")
        step("vector", [("ve", CNT["ve"])], lambda e: e.nop(), "ve")
        step("scalar", [("se", CNT["se"])], lambda e: e.nop(), "se")
        tg = {s: CNT[s] for s in ("sy", "ge", "te", "ve", "se")}
        own = {"sync": "sy", "gpsimd": "ge", "tensor": "te", "vector": "ve",
               "scalar": "se"}
        for eng, osem in own.items():
            wl = [(s, tg[s]) for s in tg if s != osem]
            step(eng, wl, lambda e: None, None)

    def flush(block):
        for eng in ENGINES:
            fns = queues[eng]
            queues[eng] = []
            if not fns:
                continue
            sec = getattr(block, eng)

            def body(e, fns=fns):
                for f in fns:
                    f(e)
            sec(body)

    blk = nc.Block()
    block = blk.__enter__()

    m_ld = step("sync", [], lambda e: e.dma_start(out=dlt[:, :], in_=deltas[:, :]),
                "Ldlt", 16)
    diffm = arf[:, 0:128]

    def p0_gp(e):
        e.iota(iof[:, :], pattern=[[1, 128]], base=0, channel_multiplier=0,
               allow_small_or_imprecise_dtypes=True)
        e.memset(onesrow[:, :], 1.0)
        e.memset(epsb[:, :], 1e-5)
        e.memset(onescol[:, :], 1.0)
        return e.iota(iop[:, :], pattern=[[0, 1]], base=0, channel_multiplier=1,
                      allow_small_or_imprecise_dtypes=True)
    m_ge = step("gpsimd", [], p0_gp, "ge")
    barrier()

    maskb = arb[:, 0:NM * 128]

    def p0_ve(e):
        e.tensor_scalar(ident[:, :], iof[:, :], iop[:, :], None, AluOpType.is_equal)
        e.drain()
        e.tensor_copy(identb[:, :], ident[:, :])
        e.tensor_scalar(stat[:, 0:1], iop[:, :], 64.0, None, AluOpType.add)
        e.drain()
        e.tensor_scalar(shiftm[:, :], iof[0:64, :], stat[0:64, 0:1], None,
                        AluOpType.is_equal)
        e.tensor_scalar(diffm, iof[:, :], iop[:, :], None, AluOpType.subtract)
        e.tensor_scalar(bdg[:, 0:1], iop[:, :], 64.0, None, AluOpType.is_lt)
        e.tensor_scalar(bdg[:, 1:2], iop[:, :], 64.0, None, AluOpType.is_ge)
        e.memset(psA[:, :], 0.0)
        e.memset(psA2[:, :], 0.0)
        for i in range(NM):
            e.tensor_scalar(maskb[:, i * 128:(i + 1) * 128], diffm,
                            dlt[:, i:i + 1], None, AluOpType.is_ge)
        return e.drain()
    step("vector", [("Ldlt", m_ld)], p0_ve, "ve")
    barrier()

    xb = [arf[:, 0:1024], arf[:, 1024:2048]]
    sqb = arf[:, 2048:3072]
    ybp = [arf[:, 3072:4096], arf[:, 4096:5120]]
    yst = [arf[:, 5120:6144], arf[:, 6144:7168]]
    wt = [arf[:, 7168:11264], arf[:, 11264:15360]]
    c1b = arf[:, 15360:19456]
    c2b = arf[:, 19456:23552]
    wqt = arb[:, 4096:8192]

    def emit_ln(src_ap, t, xbuf, ybuf, sb, prevp):
        mk = {}
        xbsem = "Lxb0" if t % 2 == 0 else "Lxb1"
        mk["ld"] = step("sync", [("ve", prevp.get("y1", 0))],
                        lambda e: e.dma_start(out=xbuf[:, :],
                                              in_=src_ap[t * 128:(t + 1) * 128, :]),
                        xbsem, 16)

        def acts(e):
            e.activation(sqb[:, :], xbuf[:, :], AF.Square,
                         accum_out=stat[:, sb:sb + 1])
            e.activation(sqb[:, :], xbuf[:, :], AF.Copy,
                         accum_out=stat[:, sb + 1:sb + 2])
            return e.drain()
        mk["acc"] = step("scalar", [(xbsem, mk["ld"]), ("ve", prevp.get("var", 0))],
                         acts, "se")

        def stats(e):
            e.tensor_scalar(stat[:, sb + 2:sb + 3], stat[:, sb + 1:sb + 2],
                            1.0 / 1024, None, AluOpType.mult)
            e.drain()
            e.tensor_tensor(stat[:, sb + 4:sb + 5], stat[:, sb + 2:sb + 3],
                            stat[:, sb + 2:sb + 3], AluOpType.mult)
            e.drain()
            e.scalar_tensor_tensor(stat[:, sb + 6:sb + 7],
                                       stat[:, sb:sb + 1], 1.0 / 1024,
                                       stat[:, sb + 4:sb + 5],
                                       AluOpType.mult, AluOpType.subtract)
            return e.drain()
        mk["var"] = step("vector", [("se", mk["acc"])], stats, "ve")
        def sqrtf(e):
            e.activation(stat[:, sb + 7:sb + 8], stat[:, sb + 6:sb + 7], AF.Sqrt,
                         bias=epsb[:, :])
            return e.drain()
        mk["sqrt"] = step("scalar", [("ve", mk["var"]), ("ve", prevp.get("y1", 0))],
                          sqrtf, "se")

        def norm(e):
            e.reciprocal(stat[:, sb + 8:sb + 9], stat[:, sb + 7:sb + 8])
            e.drain()
            e.tensor_scalar(ybuf[:, :], xbuf[:, :], stat[:, sb + 2:sb + 3],
                            stat[:, sb + 8:sb + 9],
                            AluOpType.subtract, AluOpType.mult)
            return e.drain()
        mk["y1"] = step("vector", [("se", mk["sqrt"]), ("te", prevp.get("tp", 0))],
                        norm, "ve")

        ps = psB if t % 2 == 0 else psC

        def tps(e):
            last = None
            for c in range(8):
                last = e.matmul(ps[:, c * 128:(c + 1) * 128],
                                ybuf[:, c * 128:(c + 1) * 128], ident[:, :],
                                is_transpose=True, start=True, stop=True)
            return last
        mk["tp"] = step("tensor", [("ve", mk["y1"]), ("ve", prevp.get("ev", 0))],
                        tps, "te")
        ystb = yst[t % 2]
        def evf(e):
            e.tensor_copy(ystb[:, :], ps[:, :])
            return e.drain()
        ystsem = "Syst0" if t % 2 == 0 else "Syst1"
        mk["ev"] = step("vector", [("te", mk["tp"]), (ystsem, prevp.get("st", 0))],
                        evf, "ve")
        return mk, ystb

    def ln_stream(src_ap, ntiles, dsts, prev=None):
        if prev is None:
            prev = [{}, {}]
        for t in range(ntiles):
            sb = ST_LN + (t % 2) * 13
            mk, ystb = emit_ln(src_ap, t, xb[t % 2], ybp[t % 2], sb, prev[t % 2])

            stsem = "Syst0" if t % 2 == 0 else "Syst1"

            def store(e, t=t, ystb=ystb, stsem=stsem):
                for c in range(8):
                    dld(e, stsem, dsts[c * 128:(c + 1) * 128, t * 128:(t + 1) * 128],
                        ystb[:, c * 128:(c + 1) * 128])
                return None
            mk["st"] = step("gpsimd", [("ve", mk["ev"])], store, stsem, 16 * 8)
            prev[t % 2] = mk
        return prev

    _lnp = ln_stream(x_full, 32, y1T_s)
    ln_stream(x_own, 16, y1To_s, _lnp)


    barrier()
    flush(block)

    wbuf = arf[:, 0:8192]
    ybuf = [arf[:, 8192:12288], arf[:, 12288:16384]]
    qst = [arf[:, 16384:16896], arf[:, 16896:17408]]
    sqt = [arf[:, 17408:17920], arf[:, 17920:18432]]
    vst = [arf[:, 18432:19472], arf[:, 19472:20512]]
    wtq = [arf[:, 20512:22560], arf[:, 22560:24608]]
    c1b = arf[:, 24608:26656]
    c2b = arf[:, 26656:28704]
    wqt2 = [arb[:, 4096:6144], arb[:, 6144:8192]]

    nidx = [0]
    pb_prev = {}
    pc_prev = {}
    st_prev = {}
    bd_prev = {}
    vb_prev = {}
    vst_st = {}

    def quant_units():
        carry_qz = 0
        carry_st = {}
        for (src_ap, ntiles, tile_w, dsts, mwcol) in [
                (wgT_in, 8, 4096, wgq_s, 0),
                (wuT_in, 8, 4096, wuq_s, 1),
                (wdT_in, 32, 1024, wdq_s, 2)]:
            cw = min(tile_w, 2048)
            nch = tile_w // cw
            step("vector", [], lambda e: e.memset(stat[:, ST_WQ:ST_WQ + 1], 0.0),
                 "ve")
            prev_red = [carry_qz, carry_qz]
            ui = 0
            for t in range(ntiles):
                for c in range(nch):
                    b = ui % 2
                    qsem = "Lqw0" if b == 0 else "Lqw1"
                    mld = step("scalar", [("ve", prev_red[b])],
                               lambda e, t=t, c=c, b=b, cw=cw, src_ap=src_ap: e.dma_start(
                                   out=wtq[b][:, 0:cw],
                                   in_=src_ap[t * 128:(t + 1) * 128,
                                              c * cw:(c + 1) * cw]), qsem, 16)

                    def red(e, b=b, cw=cw):
                        e.reduce_sum(stat[:, ST_WQ + 1:ST_WQ + 2], wtq[b][:, 0:cw],
                                     axis=AXX, apply_absolute_value=True)
                        e.drain()
                        e.tensor_tensor(stat[:, ST_WQ:ST_WQ + 1],
                                        stat[:, ST_WQ:ST_WQ + 1],
                                        stat[:, ST_WQ + 1:ST_WQ + 2],
                                        AluOpType.add)
                        return e.drain()
                    prev_red[b] = step("vector", [(qsem, mld)], red, "ve")
                    ui += 1
                    yield None
            mA = step("tensor", [("ve", max(prev_red)), ("ve", pc_prev.get(0, 0))],
                      lambda e: e.matmul(psC[0:1, 0:1], stat[:, ST_WQ:ST_WQ + 1],
                                         onescol[:, :], start=True, stop=True),
                      "te")

            def mk_m(e):
                e.tensor_scalar(sm[:, 8:9], psC[0:1, 0:1], 1.0 / 4194304, None,
                                AluOpType.mult)
                e.drain()
                e.tensor_scalar(sm[:, 9:10], sm[0:1, 8:9], 1e-5, None,
                                AluOpType.max)
                return e.drain()
            mB = step("vector", [("te", mA)], mk_m, "ve")
            pc_prev[0] = mB
            mC = step("tensor", [("ve", mB), ("ve", pc_prev.get(1, 0))],
                      lambda e: e.matmul(psC[:, 512:513], onesrow[0:1, :],
                                         sm[0:1, 9:10], start=True, stop=True),
                      "te")

            def thr(e, mwcol=mwcol):
                e.tensor_copy(stat[:, ST_WQ + 2:ST_WQ + 3], psC[:, 512:513])
                e.drain()
                e.tensor_scalar(stat[:, ST_WQ + 3:ST_WQ + 4],
                                stat[:, ST_WQ + 2:ST_WQ + 3], 0.5, None,
                                AluOpType.mult)
                e.drain()
                e.tensor_scalar(stat[:, ST_WQ + 4:ST_WQ + 5],
                                stat[:, ST_WQ + 3:ST_WQ + 4], -1.0, None,
                                AluOpType.mult)
                return e.tensor_scalar(mw127[:, mwcol:mwcol + 1],
                                       stat[:, ST_WQ + 2:ST_WQ + 3],
                                       1.0 / 127, None, AluOpType.mult)
            mD = step("vector", [("te", mC)], thr, "ve")
            pc_prev[1] = mD
            yield None
            prev_q = [mD, mD]
            prev_st = {}
            ui = 0
            for t in range(ntiles):
                for c in range(nch):
                    b = ui % 2
                    qsem = "Lqw0" if b == 0 else "Lqw1"
                    mld = step("scalar", [("ve", prev_q[b])],
                               lambda e, t=t, c=c, b=b, cw=cw, src_ap=src_ap: e.dma_start(
                                   out=wtq[b][:, 0:cw],
                                   in_=src_ap[t * 128:(t + 1) * 128,
                                              c * cw:(c + 1) * cw]), qsem, 16)

                    def qz(e, b=b, cw=cw):
                        e.tensor_scalar(c1b[:, 0:cw], wtq[b][:, 0:cw],
                                        stat[:, ST_WQ + 3:ST_WQ + 4], None,
                                        AluOpType.is_gt)
                        e.tensor_scalar(c2b[:, 0:cw], wtq[b][:, 0:cw],
                                        stat[:, ST_WQ + 4:ST_WQ + 5], None,
                                        AluOpType.is_lt)
                        e.drain()
                        e.tensor_tensor(wqt2[b][:, 0:cw], c1b[:, 0:cw],
                                        c2b[:, 0:cw], AluOpType.subtract)
                        return e.drain()
                    wsem = "Swqt0" if b == 0 else "Swqt1"
                    wl = [(qsem, mld), (wsem, prev_st.get(b, carry_st.get(b, 0)))]
                    mq_ = step("vector", wl, qz, "ve")
                    prev_q[b] = mq_
                    stp = step("gpsimd", [("ve", mq_)],
                               lambda e, t=t, c=c, b=b, cw=cw, dsts=dsts: e.dma_start(
                                   out=dsts[t * 128:(t + 1) * 128,
                                            c * cw:(c + 1) * cw],
                                   in_=wqt2[b][:, 0:cw]), wsem, 16)
                    prev_st[b] = stp
                    carry_st[b] = stp
                    ui += 1
                    yield None
            carry_qz = max(prev_q)

    QG = quant_units()
    qg_done = [False]

    def pump(n):
        for _ in range(n):
            if qg_done[0]:
                return
            try:
                next(QG)
            except StopIteration:
                qg_done[0] = True
                return

    def loadW(dstb, src, wait_te):
        def f(e):
            for c in range(8):
                dld(e, "Lwb", dstb[:, c * 1024:(c + 1) * 1024],
                    src[c * 128:(c + 1) * 128, :])
            return None
        return step("sync", [("te", wait_te)], f, "Lwb", 16 * 8)

    def vst_ones(e):
        for t in range(2):
            vv = vst[t].rearrange("p (h d) -> p h d", d=65)
            e.memset(vv[:, :, 64:65], 1.0)
        return e.drain()
    m_ones = step("vector", [], vst_ones, "ve")

    def proj_block(mk_deps, wbufx, ybufg, db, store_fn):
        par = db % 2
        pslice = psB[:, par * 512:(par + 1) * 512]
        wl = list(mk_deps) + [("se", pb_prev.get(par, 0))]

        def mms(e):
            last = None
            for c in range(8):
                last = e.matmul(pslice,
                                wbufx[:, c * 1024 + (db % 8) * 128:c * 1024 + (db % 8) * 128 + 128],
                                ybufg[:, c * 512:(c + 1) * 512],
                                start=(c == 0), stop=(c == 7))
            return last
        mk_mm = step("tensor", wl, mms, "te")

        def evac(e):
            e.copy(qst[par][:, :], pslice)
            e.drain()
            return e.activation(sqt[par][:, :], qst[par][:, :], AF.Square)
        qsem = "Sqst0" if par == 0 else "Sqst1"
        mk_ev = step("scalar", [("te", mk_mm), (qsem, st_prev.get(par, 0)),
                                ("te", bd_prev.get(par, 0))], evac, "se")
        pb_prev[par] = mk_ev
        st_prev[par] = step("gpsimd", [("se", mk_ev)],
                            lambda e: store_fn(e, qst[par]), qsem, 16)
        cpar = nidx[0] % 2
        cslice = psC[0:2, cpar * 512:cpar * 512 + 512]
        mk_bd = step("tensor", [("se", mk_ev), ("ve", pc_prev.get(cpar, 0))],
                     lambda e: e.matmul(cslice, bdg[:, :], sqt[par][:, :],
                                        start=True, stop=True), "te")
        bd_prev[par] = mk_bd
        col = nidx[0]
        pc_prev[cpar] = step("vector", [("te", mk_bd)],
                             lambda e: e.reduce_max(ss[0:2, col:col + 1], cslice,
                                                    axis=AXX), "ve")
        nidx[0] += 1
        return mk_mm

    mW = loadW(wbuf, wk_in, 0)
    grp_last = {}
    for g in range(8):
        ybsem = "Lyb0" if g % 2 == 0 else "Lyb1"

        def ldy(e, g=g, ybsem=ybsem):
            for c in range(8):
                dld(e, ybsem, ybuf[g % 2][:, c * 512:(c + 1) * 512],
                    y1T_s[c * 128:(c + 1) * 128, g * 512:(g + 1) * 512])
            return None
        mldy = step("sync", [("te", grp_last.get(g - 2, 0))], ldy, ybsem, 16 * 8)
        for db in range(8):
            proj_block([(ybsem, mldy), ("Lwb", mW)], wbuf, ybuf[g % 2], db,
                       lambda e, buf, g=g, db=db: e.dma_start(
                           out=KT_s[db * 128:(db + 1) * 128, g * 512:(g + 1) * 512],
                           in_=buf[:, :]))
        grp_last[g] = CNT["te"]
        pump(7)

    mW = loadW(wbuf, wv_in, CNT["te"])
    grp_last = {}
    for g in range(8):
        ybsem = "Lyb0" if g % 2 == 0 else "Lyb1"

        def ldy(e, g=g, ybsem=ybsem):
            for c in range(8):
                dld(e, ybsem, ybuf[g % 2][:, c * 512:(c + 1) * 512],
                    y1T_s[c * 128:(c + 1) * 128, g * 512:(g + 1) * 512])
            return None
        mldy = step("sync", [("te", grp_last.get(g - 2, 0))], ldy, ybsem, 16 * 8)
        for tc in range(4):
            for sl in range(2):
                idx = tc * 2 + sl
                pv = (psA if idx % 2 == 0 else psA2)[:, 0:512]
                wl = [(ybsem, mldy), ("Lwb", mW), ("ve", vb_prev.get(idx % 2, 0))]

                def vmms(e, g=g, tc=tc, sl=sl, pv=pv):
                    last = None
                    for c in range(8):
                        last = e.matmul(
                            pv,
                            ybuf[g % 2][:, c * 512 + tc * 128:c * 512 + tc * 128 + 128],
                            wbuf[:, c * 1024 + sl * 512:c * 1024 + (sl + 1) * 512],
                            start=(c == 0), stop=(c == 7))
                    return last
                mk_vm = step("tensor", wl, vmms, "te")
                vsem = "Svst0" if tc % 2 == 0 else "Svst1"
                wl2 = [("te", mk_vm), ("ve", m_ones),
                       (vsem, vst_st.get(tc % 2, 0) if sl == 0 else 0)]

                def vev(e, tc=tc, sl=sl, pv=pv):
                    vv = vst[tc % 2].rearrange("p (h d) -> p h d", d=65)
                    pvv = pv.rearrange("p (h d) -> p h d", d=64)
                    e.tensor_copy(vv[:, sl * 8:(sl + 1) * 8, 0:64], pvv)
                    return e.drain()
                mk_ve = step("vector", wl2, vev, "ve")
                vb_prev[idx % 2] = mk_ve
                if sl == 1:
                    vst_st[tc % 2] = step(
                        "gpsimd", [("ve", mk_ve)],
                        lambda e, g=g, tc=tc: e.dma_start(
                            out=V_s[g * 4 + tc, :, :], in_=vst[tc % 2][:, :]),
                        vsem, 16)
        grp_last[g] = CNT["te"]
        pump(7)

    mW = loadW(wbuf, wq_in, CNT["te"])
    grp_last = {}
    for g in range(4):
        ybsem = "Lyb0" if g % 2 == 0 else "Lyb1"

        def ldy(e, g=g, ybsem=ybsem):
            for c in range(8):
                dld(e, ybsem, ybuf[g % 2][:, c * 512:(c + 1) * 512],
                    y1To_s[c * 128:(c + 1) * 128, g * 512:(g + 1) * 512])
            return None
        mldy = step("sync", [("te", grp_last.get(g - 2, 0))], ldy, ybsem, 16 * 8)
        for db in range(8):
            proj_block([(ybsem, mldy), ("Lwb", mW)], wbuf, ybuf[g % 2], db,
                       lambda e, buf, g=g, db=db: e.dma_start(
                           out=QT_s[db * 128:(db + 1) * 128, g * 512:(g + 1) * 512],
                           in_=buf[:, :]))
        grp_last[g] = CNT["te"]
        pump(8)
    pump(1000)

    def mred(e):
        e.reduce_max(ss[0:2, 100:101], ss[0:2, 0:64], axis=AXX)
        e.reduce_max(ss[0:2, 101:102], ss[0:2, 64:96], axis=AXX)
        return e.drain()
    mk_nr = step("vector", [("ve", CNT["ve"])], mred, "ve")
    mk_tp = step("tensor", [("ve", mk_nr)],
                 lambda e: e.matmul(psC[0:2, 0:2], ss[0:2, 100:102],
                                    ident[0:2, 0:2], is_transpose=True,
                                    start=True, stop=True), "te")

    def mprod(e):
        e.tensor_copy(sm[:, 10:12], psC[0:1, 0:2])
        e.drain()
        e.tensor_tensor(sm[:, 12:13], sm[0:1, 10:11], sm[0:1, 11:12],
                        AluOpType.mult)
        return e.drain()
    mk_pr = step("vector", [("te", mk_tp)], mprod, "ve")
    def msqf(e):
        e.activation(sm[:, 13:14], sm[0:1, 12:13], AF.Sqrt, scale=1.0 / 64)
        return e.drain()
    mk_sq = step("scalar", [("ve", mk_pr)], msqf, "se")

    def mneg(e):
        e.tensor_scalar(sm[:, 14:15], sm[0:1, 13:14], 1.01, 1e-6,
                        AluOpType.mult, AluOpType.add)
        e.drain()
        e.tensor_scalar(sm[:, 15:16], sm[0:1, 14:15], -1.0, None,
                        AluOpType.mult)
        return e.drain()
    mk_ng = step("vector", [("se", mk_sq)], mneg, "ve")
    mk_bc = step("tensor", [("ve", mk_ng)],
                 lambda e: e.matmul(psC[:, 0:1], onesrow[0:1, :], sm[0:1, 15:16],
                                    start=True, stop=True), "te")
    def ngf(e):
        e.tensor_copy(negmb[:, :], psC[:, 0:1])
        return e.drain()
    step("vector", [("te", mk_bc)], ngf, "ve")
    barrier()
    flush(block)

    qtb = arf[:, 0:4096]
    ktb = [arf[:, 4096:6144], arf[:, 6144:8192]]
    vbb = [arf[:, 8192:10272], arf[:, 10272:12352]]
    Pb = [arf[:, 12352:13376], arf[:, 13376:14400]]
    srp = [arf[:, 14400:14912], arf[:, 14912:15424]]
    ACC = 15424
    accs = [arf[:, ACC + h * 512:ACC + (h + 1) * 512] for h in range(16)]
    xob = [arf[:, 23616:24640], arf[:, 24640:25664]]
    xmst = [arf[:, 25664:26688], arf[:, 26688:27712]]
    wob = arf[:, 0:8192]
    attnT = arf[:, 8192:12288]

    o_done_prev = 0
    asm_done_prev = 0

    for q in range(NQ):
        def ldq(e, q=q):
            for c in range(8):
                dld(e, "Lqt", qtb[:, c * 512:(c + 1) * 512],
                    QT_s[c * 128:(c + 1) * 128, q * 512:(q + 1) * 512])
            return None
        m_qt = step("sync", [("te", o_done_prev)], ldq, "Lqt", 16 * 8)

        def zacc(e):
            for h in range(16):
                e.memset(accs[h][0:65, :], 0.0)
            return e.nop()
        m_z = step("vector", [("te", asm_done_prev)], zacc, "ve")

        exp_mk = {}
        pv_mk = {}
        msk_mk = {}
        add_mk = {}
        sc_mk = {}
        kg_ld = {}

        def prevmk(dd, kg, h, delta=2):
            if (kg, h - delta) in dd:
                return dd[(kg, h - delta)]
            return dd.get((kg - 1, h + 16 - delta), 0)

        for kg in range(KGQ[q]):
            kvsem = "Lkv0" if kg % 2 == 0 else "Lkv1"

            def ldkv(e, q=q, kg=kg, kvsem=kvsem):
                for c in range(8):
                    dld(e, kvsem, ktb[kg % 2][:, c * 256:(c + 1) * 256],
                        KT_s[c * 128:(c + 1) * 128, kg * 256:(kg + 1) * 256])
                for i in range(2):
                    dld(e, kvsem, vbb[kg % 2][:, i * 1040:(i + 1) * 1040],
                        V_s[kg * 2 + i, :, :])
                return None
            if kg >= 2:
                wl = [("te", pv_mk[(kg - 2, 15)])]
            else:
                wl = [("te", o_done_prev)]
            kg_ld[kg] = step("sync", wl, ldkv, kvsem, 16 * 10)

            def emit_scores(h):
                psS = psA if h % 2 == 0 else psA2
                kcs = [2 * kg, 2 * kg + 1]
                sfx = [_suffix(q, kc) for kc in kcs]

                def scores(e, kg=kg, h=h, kcs=kcs, sfx=sfx, psS=psS):
                    last = None
                    for i in range(2):
                        s = sfx[i]
                        last = e.matmul(
                            psS[:, i * 512 + s * 128:(i + 1) * 512],
                            ktb[kg % 2][(h % 2) * 64:(h % 2) * 64 + 64,
                                        (h // 2) * 256 + i * 128:(h // 2) * 256 + (i + 1) * 128],
                            qtb[(h % 2) * 64:(h % 2) * 64 + 64,
                                (h // 2) * 512 + s * 128:(h // 2) * 512 + 512],
                            start=True, stop=True)
                    return last
                kvsem = "Lkv0" if kg % 2 == 0 else "Lkv1"
                wl = [(kvsem, kg_ld[kg]), ("Lqt", m_qt),
                      ("se", prevmk(exp_mk, kg, h))]
                sc_mk[(kg, h)] = step("tensor", wl, scores, "te")

                def expf(e, h=h, psS=psS):
                    e.activation(Pb[h % 2][:, :], psS[:, :], AF.Exp,
                                 bias=negmb[:, :], scale=0.125)
                    return e.drain()
                wl = [("te", sc_mk[(kg, h)]), ("te", prevmk(pv_mk, kg, h))]
                exp_mk[(kg, h)] = step("scalar", wl, expf, "se")

                mts = []
                for i, kc in enumerate(kcs):
                    for r in range(sfx[i], 4):
                        if (q, kc, r) in MIDX:
                            mts.append((i, r, MIDX[(q, kc, r)]))
                if mts:
                    def maskf(e, h=h, mts=mts):
                        last = None
                        for i, r, idx in mts:
                            sl = Pb[h % 2][:, i * 512 + r * 128:i * 512 + (r + 1) * 128]
                            last = e.tensor_tensor(
                                sl, sl, maskb[:, idx * 128:(idx + 1) * 128],
                                AluOpType.mult)
                        return e.drain()
                    msk_mk[(kg, h)] = step("vector", [("se", exp_mk[(kg, h)])],
                                           maskf, "ve")

            def emit_pv(h):
                kcs = [2 * kg, 2 * kg + 1]
                sfx = [_suffix(q, kc) for kc in kcs]

                def pvf(e, kg=kg, h=h, sfx=sfx):
                    last = None
                    for i in range(2):
                        s = sfx[i]
                        last = e.matmul(
                            psB[0:65, (h % 2) * 512 + s * 128:(h % 2) * 512 + 512],
                            vbb[kg % 2][:, i * 1040 + h * 65:i * 1040 + h * 65 + 65],
                            Pb[h % 2][:, i * 512 + s * 128:(i + 1) * 512],
                            start=(i == 0), stop=(i == 1))
                    return last
                wl = [("se", exp_mk[(kg, h)]), ("ve", prevmk(add_mk, kg, h)),
                      ("ve", m_z)]
                if (kg, h) in msk_mk:
                    wl.append(("ve", msk_mk[(kg, h)]))
                pv_mk[(kg, h)] = step("tensor", wl, pvf, "te")

                sg = sfx[0]

                def addf(e, kg=kg, h=h, sg=sg):
                    sl = accs[h][0:65, sg * 128:512]
                    return e.tensor_tensor(
                        sl, sl,
                        psB[0:65, (h % 2) * 512 + sg * 128:(h % 2) * 512 + 512],
                        AluOpType.add)
                add_mk[(kg, h)] = step("vector", [("te", pv_mk[(kg, h)])], addf,
                                       "ve")

            for h in range(17):
                if h < 16:
                    emit_scores(h)
                if h >= 1:
                    emit_pv(h - 1)

        if "p3kg" in SER:
            barrier()
        last_kg = KGQ[q] - 1

        def ldwo(e):
            for c in range(8):
                dld(e, "Lwo", wob[:, c * 1024:(c + 1) * 1024],
                    wo_in[c * 128:(c + 1) * 128, :])
            return None
        m_wo = step("sync", [("te", pv_mk[(last_kg, 15)])], ldwo, "Lwo", 16 * 8)

        add_last = add_mk[(last_kg, 15)]
        normPs = [psB[:, 0:512], psB[:, 512:1024], psA[:, 0:512]]
        asmPs = [psC[:, 0:512], psC[:, 512:1024], psA2[:, 0:512]]
        asm_mk = {}
        bc_mk = {}
        rcp_mk = {}
        at_mk = {}
        for j in range(8):
            def rcpf(e, j=j):
                e.reciprocal(arf[0:1, 12352 + (j % 3) * 1024:
                                  12352 + (j % 3) * 1024 + 1024],
                             arf[64:65, ACC + 2 * j * 512:
                                 ACC + 2 * j * 512 + 1024])
                return e.drain()
            rcp_mk[j] = step("vector", [("ve", add_last),
                                        ("te", bc_mk.get(j - 3, 0))], rcpf, "ve")

            def bcf(e, j=j):
                base = 12352 + (j % 3) * 1024
                ps = normPs[j % 3]
                last = None
                for i in range(2):
                    last = e.matmul(
                        ps[i * 64:(i + 1) * 64, :],
                        onesrow[0:1, 0:64],
                        arf[0:1, base + i * 512:base + (i + 1) * 512],
                        start=True, stop=True)
                return last
            bc_mk[j] = step("tensor",
                            [("ve", rcp_mk[j]), ("ve", at_mk.get(j - 3, 0))],
                            bcf, "te")

            def asmf(e, j=j):
                ps = asmPs[j % 3]
                e.matmul(ps, ident[0:64, :], accs[2 * j][0:64, :],
                         start=True, stop=False)
                return e.matmul(ps, shiftm[:, :], accs[2 * j + 1][0:64, :],
                                start=False, stop=True)
            asm_mk[j] = step("tensor", [("ve", add_last)], asmf, "te")

            def atf1(e, j=j):
                e.copy(attnT[:, j * 512:(j + 1) * 512], asmPs[j % 3])
                return e.drain()
            at_se = step("scalar", [("te", asm_mk[j])], atf1, "se")

            def atf2(e, j=j):
                sl = attnT[:, j * 512:(j + 1) * 512]
                e.tensor_tensor(sl, sl, normPs[j % 3], AluOpType.mult)
                return e.drain()
            at_mk[j] = step("vector", [("se", at_se), ("te", bc_mk[j])],
                            atf2, "ve")

        xev_mk = {}
        o_mk = {}
        xst_mk = {}
        for tsub in range(4):
            xosem = "Lxo0" if tsub % 2 == 0 else "Lxo1"
            xo_mk = step("scalar", [("ve", xev_mk.get((tsub - 2, 1), 0))],
                         lambda e, q=q, tsub=tsub: e.dma_start(
                             out=xob[tsub % 2][:, :],
                             in_=x_own[(q * 4 + tsub) * 128:(q * 4 + tsub + 1) * 128, :]),
                         xosem, 16)
            for osl in range(2):
                pidx = (tsub * 2 + osl) % 2

                def omms(e, tsub=tsub, osl=osl, pidx=pidx):
                    last = None
                    for j in range(8):
                        last = e.matmul(
                            psC[:, pidx * 512:pidx * 512 + 512],
                            attnT[:, j * 512 + tsub * 128:j * 512 + tsub * 128 + 128],
                            wob[:, j * 1024 + osl * 512:j * 1024 + (osl + 1) * 512],
                            start=(j == 0), stop=(j == 7))
                    return last
                wl = [("ve", at_mk[7]), ("Lwo", m_wo),
                      ("ve", xev_mk.get((tsub - 1, osl), 0))]
                o_mk[(tsub, osl)] = step("tensor", wl, omms, "te")

                def xevf(e, tsub=tsub, osl=osl, pidx=pidx):
                    e.tensor_tensor(
                        xmst[tsub % 2][:, osl * 512:(osl + 1) * 512],
                        psC[:, pidx * 512:pidx * 512 + 512],
                        xob[tsub % 2][:, osl * 512:(osl + 1) * 512],
                        AluOpType.add)
                    return e.drain()
                xmsem = "Sxm0" if tsub % 2 == 0 else "Sxm1"
                wl = [("te", o_mk[(tsub, osl)]), (xosem, xo_mk),
                      (xmsem, xst_mk.get(tsub - 2, 0))]
                xev_mk[(tsub, osl)] = step("vector", wl, xevf, "ve")
            xmsem = "Sxm0" if tsub % 2 == 0 else "Sxm1"
            xst_mk[tsub] = step("gpsimd",
                                [("ve", xev_mk[(tsub, 0)]),
                                 ("ve", xev_mk[(tsub, 1)])],
                                lambda e, q=q, tsub=tsub: e.dma_start(
                                    out=xmid_s[(q * 4 + tsub) * 128:(q * 4 + tsub + 1) * 128, :],
                                    in_=xmst[tsub % 2][:, :]), xmsem, 16)
        if "p3tail" in SER:
            barrier()
        o_done_prev = o_mk[(3, 1)]
        asm_done_prev = asm_mk[7]
        flush(block)

    barrier()

    gh = [arf[:, i * 4096:(i + 1) * 4096] for i in range(4)]
    dumb = arf[:, 22528:23552]
    corr1 = arf[:, 23552:24576]
    corr2 = arf[:, 24576:25600]
    y2b = arf[:, 16384:17408]
    xmld = [arf[:, 17408:18432], arf[:, 18432:19456]]
    xout = arf[:, 19456:20480]
    xmr = [arf[:, 20480:21504], arf[:, 21504:22528]]
    y2qT = arb[:, 0:4096]
    wgb = [arb[:, 4096:8192], arb[:, 8192:12288]]
    wub = [arb[:, 12288:16384], arb[:, 16384:20480]]
    hqT = arb[:, 4096:20480]
    wdb = [arb[:, 20480:24576], arb[:, 24576:28672]]
    hqs = arb[:, 28672:29184]
    y2qb = arb[:, 29184:30208]

    if DEBUG2:
        def dbgdump2(e):
            e.dma_start(out=dbg_xmid2, in_=xmid_s[:, :]).then_inc(SEM["st"], 16)
            e.dma_start(out=dbg_wgq, in_=wgq_s[:, :]).then_inc(SEM["st"], 16)
            e.dma_start(out=dbg_wuq, in_=wuq_s[:, :]).then_inc(SEM["st"], 16)
            e.dma_start(out=dbg_wdq, in_=wdq_s[:, :]).then_inc(SEM["st"], 16)
            return None
        step("gpsimd", [], dbgdump2, "st", 64)
    if DEBUG:
        def dbgdump(e):
            e.dma_start(out=dbg_stat[:, 0:56], in_=stat[:, :]).then_inc(SEM["st"], 16)
            e.dma_start(out=dbg_stat[:, 56:57], in_=negmb[:, :], allow_slow_non_contiguous=True).then_inc(SEM["st"], 16)
            e.dma_start(out=dbg_stat[:, 57:60], in_=mw127[:, :], allow_slow_non_contiguous=True).then_inc(SEM["st"], 16)
            e.dma_start(out=dbg_stat[0:2, 60:62], in_=ss[:, 100:102], allow_slow_non_contiguous=True).then_inc(SEM["st"], 16)
            e.dma_start(out=dbg_stat[0:1, 62:63], in_=sm[:, 15:16], allow_slow_non_contiguous=True).then_inc(SEM["st"], 16)
            e.dma_start(out=dbg_xmid, in_=xmid_s[:, :]).then_inc(SEM["st"], 16)
            e.dma_start(out=dbg_y1T, in_=y1T_s[:, :]).then_inc(SEM["st"], 16)
            e.dma_start(out=dbg_KT, in_=KT_s[:, 0:512]).then_inc(SEM["st"], 16)
            e.dma_start(out=dbg_QT, in_=QT_s[:, 0:512]).then_inc(SEM["st"], 16)
            e.dma_start(out=dbg_P, in_=Pb[0][:, :]).then_inc(SEM["st"], 16)
            return None
        step("gpsimd", [], dbgdump, "st", 160)
    dn_ev_prev = 0
    dn_st_prev = 0
    dn23_prev = 0
    ln_prev = {}
    for tt in range(4):
        dprev = dn_ev_prev
        sprev = dn_st_prev
        y2tp_mk = {}
        ln_mk = {}
        tp_mk = {}
        for sub in range(4):
            sb = ST_LN + (sub % 2) * 13
            xlsem = "Lxl0" if sub % 2 == 0 else "Lxl1"
            mld = step("scalar", [("ve", ln_mk.get(sub - 2, ln_prev.get(sub + 2, 0)))],
                       lambda e, tt=tt, sub=sub: e.dma_start(
                           out=xmld[sub % 2][:, :],
                           in_=xmid_s[(tt * 4 + sub) * 128:(tt * 4 + sub + 1) * 128, :]),
                       xlsem, 16)

            def acts(e, sub=sub, sb=sb):
                e.activation(dumb[:, :], xmld[sub % 2][:, :], AF.Square,
                             accum_out=stat[:, sb:sb + 1])
                e.activation(dumb[:, :], xmld[sub % 2][:, :], AF.Copy,
                             accum_out=stat[:, sb + 1:sb + 2])
                return e.drain()
            macc = step("scalar", [(xlsem, mld),
                                   ("ve", ln_mk.get(sub - 2, ln_prev.get(sub + 2, 0)))],
                        acts, "se")

            def stats(e, sb=sb):
                e.tensor_scalar(stat[:, sb + 2:sb + 3], stat[:, sb + 1:sb + 2],
                                1.0 / 1024, None, AluOpType.mult)
                e.drain()
                e.tensor_tensor(stat[:, sb + 4:sb + 5], stat[:, sb + 2:sb + 3],
                                stat[:, sb + 2:sb + 3], AluOpType.mult)
                e.drain()
                e.scalar_tensor_tensor(stat[:, sb + 6:sb + 7],
                                       stat[:, sb:sb + 1], 1.0 / 1024,
                                       stat[:, sb + 4:sb + 5],
                                       AluOpType.mult, AluOpType.subtract)
                return e.drain()
            mvar = step("vector", [("se", macc)], stats, "ve")
            def msqf2(e, sb=sb):
                e.activation(stat[:, sb + 7:sb + 8], stat[:, sb + 6:sb + 7],
                             AF.Sqrt, bias=epsb[:, :])
                return e.drain()
            msq = step("scalar", [("ve", mvar)], msqf2, "se")

            def nq(e, sub=sub, sb=sb):
                e.reciprocal(stat[:, sb + 8:sb + 9], stat[:, sb + 7:sb + 8])
                e.drain()
                e.tensor_scalar(y2b[:, :], xmld[sub % 2][:, :],
                                stat[:, sb + 2:sb + 3], stat[:, sb + 8:sb + 9],
                                AluOpType.subtract, AluOpType.mult)
                e.reduce_max(stat[:, ST_YQ:ST_YQ + 1], y2b[:, :], axis=AXX,
                             apply_absolute_value=True)
                e.drain()
                e.tensor_scalar(stat[:, ST_YQ + 1:ST_YQ + 2],
                                stat[:, ST_YQ:ST_YQ + 1], 1e-5, None, AluOpType.max)
                e.drain()
                e.reciprocal(stat[:, ST_YQ + 2:ST_YQ + 3],
                             stat[:, ST_YQ + 1:ST_YQ + 2])
                e.drain()
                e.tensor_scalar(stat[:, ST_YQ + 3:ST_YQ + 4],
                                stat[:, ST_YQ + 2:ST_YQ + 3], 127.0, None,
                                AluOpType.mult)
                e.tensor_scalar(stat[:, ST_CTG + sub:ST_CTG + sub + 1],
                                stat[:, ST_YQ + 1:ST_YQ + 2], mw127[:, 0:1], None,
                                AluOpType.mult)
                e.tensor_scalar(stat[:, ST_CTU + sub:ST_CTU + sub + 1],
                                stat[:, ST_YQ + 1:ST_YQ + 2], mw127[:, 1:2], None,
                                AluOpType.mult)
                e.drain()
                e.tensor_scalar(y2b[:, :], y2b[:, :], stat[:, ST_YQ + 3:ST_YQ + 4],
                                None, AluOpType.mult)
                e.tensor_copy(i32y[:, :], y2b[:, :])
                e.tensor_copy(corr1[:, :], i32y[:, :])
                e.tensor_tensor(y2b[:, :], y2b[:, :], corr1[:, :], AluOpType.subtract)
                e.tensor_scalar(corr2[:, :], y2b[:, :], 0.5, None, AluOpType.is_gt)
                e.tensor_scalar(y2b[:, :], y2b[:, :], -0.5, None, AluOpType.is_lt)
                e.tensor_tensor(corr2[:, :], corr2[:, :], y2b[:, :],
                                AluOpType.subtract)
                e.tensor_tensor(y2b[:, :], corr1[:, :], corr2[:, :],
                                AluOpType.add)
                return e.drain()
            mq_ = step("vector", [("se", msq), ("te", tp_mk.get(sub - 1, 0))], nq,
                       "ve")
            ln_mk[sub] = mq_

            psT = psB if sub % 2 == 0 else psC

            def tps(e, psT=psT):
                last = None
                for c in range(8):
                    last = e.matmul(psT[:, c * 128:(c + 1) * 128],
                                    y2b[:, c * 128:(c + 1) * 128], ident[:, :],
                                    is_transpose=True, start=True, stop=True)
                return last
            mtp = step("tensor", [("ve", mq_), ("ve", y2tp_mk.get(sub - 2, dprev))],
                       tps, "te")
            tp_mk[sub] = mtp
            def y2tpf(e, sub=sub, psT=psT):
                e.tensor_copy(y2qT[:, sub * 1024:(sub + 1) * 1024], psT[:, :])
                return e.drain()
            y2tp_mk[sub] = step("vector", [("te", mtp)], y2tpf, "ve")
            if DEBUG and tt == 0 and sub == 3:
                step("gpsimd", [("ve", y2tp_mk[sub])],
                     lambda e: e.dma_start(out=dbg_y2qT, in_=y2qT[:, 0:4096]), "st", 16)

        if "p4ln" in SER:
            barrier()
        gu_te = {}
        gu_se = {}
        gu_ve = {}
        psGU = [psA[:, 0:512], psA[:, 512:1024], psA2[:, 0:512], psA2[:, 512:1024]]
        gidx = 0
        for isl in range(8):
            wgsem = "Lwg0" if isl % 2 == 0 else "Lwg1"

            def ldwgu(e, isl=isl, wgsem=wgsem):
                for c in range(8):
                    dld(e, wgsem, wgb[isl % 2][:, c * 512:(c + 1) * 512],
                        wgq_s[c * 128:(c + 1) * 128, isl * 512:(isl + 1) * 512])
                    dld(e, wgsem, wub[isl % 2][:, c * 512:(c + 1) * 512],
                        wuq_s[c * 128:(c + 1) * 128, isl * 512:(isl + 1) * 512])
                return None
            wld = step("sync", [("te", gu_te.get(isl - 2, dn23_prev))], ldwgu,
                       wgsem, 16 * 16)
            for sub in range(4):
                pg = psGU[gidx % 4]
                wl = [(wgsem, wld), ("ve", y2tp_mk[sub]),
                      ("se", gu_se.get(gidx - 4, 0)), ("ve", gu_ve.get(gidx - 4, 0))]
                if gidx < 4:
                    wl.append(("ve", dprev))

                def gmms(e, isl=isl, sub=sub, pg=pg):
                    last = None
                    for c in range(8):
                        last = e.matmul(
                            pg,
                            y2qT[:, sub * 1024 + c * 128:sub * 1024 + (c + 1) * 128],
                            wgb[isl % 2][:, c * 512:(c + 1) * 512],
                            start=(c == 0), stop=(c == 7))
                    return last
                mg = step("tensor", wl, gmms, "te")
                def siluf(e, isl=isl, sub=sub, pg=pg):
                    e.activation(gh[sub][:, isl * 512:(isl + 1) * 512], pg, AF.Silu,
                                 scale=stat[:, ST_CTG + sub:ST_CTG + sub + 1])
                    return e.drain()
                mge = step("scalar", [("te", mg)], siluf, "se")
                gu_se[gidx] = mge
                pu = psGU[(gidx + 1) % 4]
                wl = [("se", gu_se.get(gidx - 3, 0)), ("ve", gu_ve.get(gidx - 3, 0))]
                if gidx < 3:
                    wl.append(("ve", dprev))

                def umms(e, isl=isl, sub=sub, pu=pu):
                    last = None
                    for c in range(8):
                        last = e.matmul(
                            pu,
                            y2qT[:, sub * 1024 + c * 128:sub * 1024 + (c + 1) * 128],
                            wub[isl % 2][:, c * 512:(c + 1) * 512],
                            start=(c == 0), stop=(c == 7))
                    return last
                mu_ = step("tensor", wl, umms, "te")
                gu_te[isl] = mu_

                def ustt(e, isl=isl, sub=sub, pu=pu):
                    sl = gh[sub][:, isl * 512:(isl + 1) * 512]
                    return e.scalar_tensor_tensor(
                        sl, pu, stat[:, ST_CTU + sub:ST_CTU + sub + 1], sl,
                        AluOpType.mult, AluOpType.mult)
                gu_ve[gidx] = step("vector", [("te", mu_), ("se", mge)], ustt, "ve")
                gidx += 2
        if "p4gu" in SER:
            barrier()
        all_gu_te = CNT["te"]
        all_gu_se = CNT["se"]
        all_gu_ve = CNT["ve"]
        dbg_gh_st = 0
        if DEBUG and tt == 0:
            dbg_gh_st = step("gpsimd", [("ve", all_gu_ve), ("se", all_gu_se)],
                             lambda e: e.dma_start(out=dbg_gh, in_=gh[0][:, :]),
                             "st", 16)

        msc_mk = {}
        for sub in range(4):
            def hscale(e, sub=sub):
                e.reduce_max(stat[:, 51:52], gh[sub][:, :], axis=AXX,
                             apply_absolute_value=True)
                e.drain()
                e.tensor_scalar(stat[:, 52:53], stat[:, 51:52], 1e-5, None,
                                AluOpType.max)
                e.drain()
                e.reciprocal(stat[:, 53:54], stat[:, 52:53])
                e.drain()
                e.tensor_scalar(stat[:, ST_HQ + sub:ST_HQ + sub + 1],
                                stat[:, 53:54], 127.0, None, AluOpType.mult)
                e.tensor_scalar(stat[:, ST_C2 + sub:ST_C2 + sub + 1],
                                stat[:, 52:53], mw127[:, 2:3],
                                None, AluOpType.mult)
                return e.drain()
            msc_mk[sub] = step("vector", [("ve", all_gu_ve)], hscale, "ve")

        fbuf = arf[:, 25600:27648]
        hw_mk = {}
        tp_mk2 = {}
        ev_mk = {}
        hcnt = 0
        for sub in range(4):
            for half in range(2):
                def chainf(e, sub=sub, half=half):
                    sl = gh[sub][:, half * 2048:(half + 1) * 2048]
                    e.tensor_scalar(sl, sl, stat[:, ST_HQ + sub:ST_HQ + sub + 1],
                                    None, AluOpType.mult)
                    e.tensor_copy(i32t[:, :], sl)
                    e.tensor_copy(fbuf[:, :], i32t[:, :])
                    e.tensor_tensor(sl, sl, fbuf[:, :], AluOpType.subtract)
                    e.scalar_tensor_tensor(fbuf[:, :], sl, 0.5, fbuf[:, :],
                                           AluOpType.is_gt, AluOpType.add)
                    e.tensor_scalar(sl, sl, -0.5, None, AluOpType.is_lt)
                    e.tensor_tensor(sl, fbuf[:, :], sl, AluOpType.subtract)
                    return e.drain()
                cwl = [("ve", msc_mk[sub])]
                if sub == 0 and half == 0:
                    cwl.append(("st", dbg_gh_st))
                hw_mk[(sub, half)] = step("vector", cwl, chainf, "ve")
                for cc in range(4):
                    psT = psB[:, (hcnt % 2) * 512:(hcnt % 2) * 512 + 512]
                    isl = half * 4 + cc

                    def hq2(e, sub=sub, isl=isl, psT=psT):
                        last = None
                        for c in range(4):
                            last = e.matmul(
                                psT[:, c * 128:(c + 1) * 128],
                                gh[sub][:, isl * 512 + c * 128:isl * 512 + (c + 1) * 128],
                                ident[:, :],
                                is_transpose=True, start=True, stop=True)
                        return last
                    tp_mk2[hcnt] = step("tensor",
                                        [("ve", hw_mk[(sub, half)]),
                                         ("se", ev_mk.get(hcnt - 2, 0)),
                                         ("ve", dprev if hcnt == 0 else 0)],
                                        hq2, "te")

                    def hq3(e, sub=sub, isl=isl, psT=psT):
                        off = sub * 4096 + isl * 512
                        e.activation(hqT[:, off:off + 512], psT, AF.Copy)
                        return e.drain()
                    ev_mk[hcnt] = step("scalar", [("te", tp_mk2[hcnt])], hq3,
                                       "se")
                    hcnt += 1

        all_tp_ve = CNT["ve"]
        all_tp_se = CNT["se"]
        psDN = [[psA[:, 0:512], psA[:, 512:1024]],
                [psA2[:, 0:512], psA2[:, 512:1024]],
                [psC[:, 0:512], psC[:, 512:1024]],
                [psB[:, 0:512], psB[:, 512:1024]]]
        dn23_te = {}
        dwld2 = {}
        for icg in range(8):
            wdsem = "Lwd0" if icg % 2 == 0 else "Lwd1"

            def lddw2(e, icg=icg, wdsem=wdsem):
                for cc in range(4):
                    dld(e, wdsem, wdb[icg % 2][:, cc * 1024:(cc + 1) * 1024],
                        wdq_s[icg * 512 + cc * 128:icg * 512 + (cc + 1) * 128, :])
                return None
            wl = [("te", dn23_te.get(icg - 2, dn23_prev if icg < 2 else 0))]
            dwld2[icg] = step("sync", wl, lddw2, wdsem, 16 * 4)
            wl = [(wdsem, dwld2[icg]), ("ve", all_tp_ve), ("se", all_tp_se)]

            def dmms23(e, icg=icg):
                last = None
                for sub in range(4):
                    for ic4 in range(4):
                        for osl in range(2):
                            last = e.matmul(
                                psDN[sub][osl],
                                hqT[:, sub * 4096 + (icg * 4 + ic4) * 128:
                                    sub * 4096 + (icg * 4 + ic4) * 128 + 128],
                                wdb[icg % 2][:, ic4 * 1024 + osl * 512:
                                             ic4 * 1024 + (osl + 1) * 512],
                                start=(icg == 0 and ic4 == 0),
                                stop=(icg == 7 and ic4 == 3))
                return last
            dn23_te[icg] = step("tensor", wl, dmms23, "te")

        dn_ev = {}
        dn_st = {}
        for sub in range(4):
            xrsem = "Lxr0" if sub % 2 == 0 else "Lxr1"
            mxr = step("scalar", [("ve", dn_ev.get(sub - 2, dprev))],
                       lambda e, tt=tt, sub=sub: e.dma_start(
                           out=xmr[sub % 2][:, :],
                           in_=xmid_s[(tt * 4 + sub) * 128:(tt * 4 + sub + 1) * 128, :]),
                       xrsem, 16)

            def dnev(e, sub=sub):
                for osl in range(2):
                    e.scalar_tensor_tensor(
                        xout[:, osl * 512:(osl + 1) * 512],
                        psDN[sub][osl],
                        stat[:, ST_C2 + sub:ST_C2 + sub + 1],
                        xmr[sub % 2][:, osl * 512:(osl + 1) * 512],
                        AluOpType.mult, AluOpType.add)
                return e.drain()
            wl = [("te", dn23_te[7]), (xrsem, mxr), ("Sy", dn_st.get(sub - 1, sprev))]
            dn_ev[sub] = step("vector", wl, dnev, "ve")
            dn_st[sub] = step("gpsimd", [("ve", dn_ev[sub])],
                              lambda e, tt=tt, sub=sub: e.dma_start(
                                  out=y_own[(tt * 4 + sub) * 128:(tt * 4 + sub + 1) * 128, :],
                                  in_=xout[:, :]), "Sy", 16)
        if "p4dn" in SER:
            barrier()
        dn_ev_prev = dn_ev[3]
        dn_st_prev = dn_st[3]
        dn23_prev = dn23_te[7]
        ln_prev = ln_mk
        flush(block)

    step("gpsimd", [(n, CNT[n]) for n in SSEMS], lambda e: e.nop(), "ge")
    flush(block)
    blk.__exit__(None, None, None)
    return nc


_NC = None


def kernel(**inputs):
    global _NC
    x = np.asarray(inputs["L0_attn_x"], dtype=np.float32)
    wq = np.ascontiguousarray(np.asarray(inputs["L0_attn_W_Q"], np.float32))
    wk = np.ascontiguousarray(np.asarray(inputs["L0_attn_W_K"], np.float32))
    wv = np.ascontiguousarray(np.asarray(inputs["L0_attn_W_V"], np.float32))
    wo = np.ascontiguousarray(np.asarray(inputs["L0_attn_W_o"], np.float32))
    wgT = np.ascontiguousarray(np.asarray(inputs["L0_ffn_w_gate"], np.float32).T)
    wuT = np.ascontiguousarray(np.asarray(inputs["L0_ffn_w_up"], np.float32).T)
    wdT = np.ascontiguousarray(np.asarray(inputs["L0_ffn_w_down"], np.float32).T)

    blocks = (B0, B1)
    dl = {}
    for par in range(2):
        d = np.zeros((128, NM), np.float32)
        for i, (q, kc, r) in enumerate(MASKS):
            g = blocks[par][4 * q + r]
            d[:, i] = 128.0 * (kc - g)
        dl[par] = d

    in_maps = []
    for c in range(8):
        seq, par = c // 2, c % 2
        rows = np.concatenate([np.arange(b * 128, (b + 1) * 128)
                               for b in blocks[par]])
        in_maps.append({
            "x_full": np.ascontiguousarray(x[seq]),
            "x_own": np.ascontiguousarray(x[seq][rows]),
            "deltas": dl[par],
            "wq_in": wq, "wk_in": wk, "wv_in": wv, "wo_in": wo,
            "wgT_in": wgT, "wuT_in": wuT, "wdT_in": wdT,
        })

    if _NC is None:
        _NC = build_nc()
    res = run_bass_kernel_spmd(_NC, in_maps, core_ids=list(range(8)))

    y = np.empty((4, S, D), np.float32)
    for c in range(8):
        seq, par = c // 2, c % 2
        out = np.asarray(res.results[c]["y_own"])
        for r, b in enumerate(blocks[par]):
            y[seq, b * 128:(b + 1) * 128, :] = out[r * 128:(r + 1) * 128, :]
    return y


if __name__ == "__main__":
    import time
    t0 = time.time()
    nc = build_nc()
    print(f"build ok in {time.time() - t0:.1f}s")


# version 20 (final)
# speedup vs baseline: 1.0854x; 1.0854x over previous
import numpy as np
import concourse.bass as bass
import concourse.mybir as mybir
from concourse.bass_utils import run_bass_kernel_spmd
from concourse.alu_op_type import AluOpType

F32 = mybir.dt.float32
BF16 = mybir.dt.bfloat16
I32 = mybir.dt.int32
AF = mybir.ActivationFunctionType
AXX = mybir.AxisListType.X

D = 1024
S = 4096
OT = 2048
NQ = 4
ENGINES = ("sync", "gpsimd", "tensor", "vector", "scalar")
DEBUG = False
DEBUG2 = False
SER = set()


def _blocks(par):
    bs = []
    for j in range(16):
        if j % 2 == par:
            bs.append(j)
            bs.append(31 - j)
    return sorted(bs)


B0 = _blocks(0)
B1 = _blocks(1)
MAXG = [max(B0[r], B1[r]) for r in range(16)]
MING = [min(B0[r], B1[r]) for r in range(16)]
KCQ = [MAXG[4 * q + 3] + 1 for q in range(NQ)]
KGQ = [kc // 2 for kc in KCQ]


def _suffix(q, kc):
    for r in range(4):
        if MAXG[4 * q + r] >= kc:
            return r
    raise AssertionError


MASKS = []
MIDX = {}
for _q in range(NQ):
    for _r in range(4):
        for _kc in range(MING[4 * _q + _r], MAXG[4 * _q + _r] + 1):
            MIDX[(_q, _kc, _r)] = len(MASKS)
            MASKS.append((_q, _kc, _r))
NM = len(MASKS)
assert NM == 32, NM

ST_LN = 4
ST_WQ = 30
ST_CTG = 35
ST_CTU = 39
ST_C2 = 43
ST_HQ = 47
ST_YQ = 51


def build_nc():
    nc = bass.Bass("TRN2", target_bir_lowering=False, detect_race_conditions=False)

    x_full = nc.dram_tensor("x_full", [S, D], F32, kind="ExternalInput").ap()
    x_own = nc.dram_tensor("x_own", [OT, D], F32, kind="ExternalInput").ap()
    deltas = nc.dram_tensor("deltas", [128, NM], F32, kind="ExternalInput").ap()
    wq_in = nc.dram_tensor("wq_in", [D, D], F32, kind="ExternalInput").ap()
    wk_in = nc.dram_tensor("wk_in", [D, D], F32, kind="ExternalInput").ap()
    wv_in = nc.dram_tensor("wv_in", [D, D], F32, kind="ExternalInput").ap()
    wo_in = nc.dram_tensor("wo_in", [D, D], F32, kind="ExternalInput").ap()
    wgT_in = nc.dram_tensor("wgT_in", [D, 4096], F32, kind="ExternalInput").ap()
    wuT_in = nc.dram_tensor("wuT_in", [D, 4096], F32, kind="ExternalInput").ap()
    wdT_in = nc.dram_tensor("wdT_in", [4096, D], F32, kind="ExternalInput").ap()
    y_own = nc.dram_tensor("y_own", [OT, D], F32, kind="ExternalOutput").ap()

    y1T_s = nc.dram_tensor("y1T_s", [D, S], F32, kind="Internal").ap()
    y1To_s = nc.dram_tensor("y1To_s", [D, OT], F32, kind="Internal").ap()
    KT_s = nc.dram_tensor("KT_s", [D, S], F32, kind="Internal").ap()
    V_s = nc.dram_tensor("V_s", [32, 128, 1040], F32, kind="Internal").ap()
    QT_s = nc.dram_tensor("QT_s", [D, OT], F32, kind="Internal").ap()
    xmid_s = nc.dram_tensor("xmid_s", [OT, D], F32, kind="Internal").ap()
    wgq_s = nc.dram_tensor("wgq_s", [D, 4096], BF16, kind="Internal").ap()
    if DEBUG2:
        dbg_xmid2 = nc.dram_tensor("dbg_xmid2", [OT, D], F32, kind="ExternalOutput").ap()
        dbg_wgq = nc.dram_tensor("dbg_wgq", [D, 4096], BF16, kind="ExternalOutput").ap()
        dbg_wuq = nc.dram_tensor("dbg_wuq", [D, 4096], BF16, kind="ExternalOutput").ap()
        dbg_wdq = nc.dram_tensor("dbg_wdq", [4096, D], BF16, kind="ExternalOutput").ap()
    if DEBUG:
        dbg_y2qT = nc.dram_tensor("dbg_y2qT", [128, 4096], BF16, kind="ExternalOutput").ap()
        dbg_gh = nc.dram_tensor("dbg_gh", [128, 4096], F32, kind="ExternalOutput").ap()
        dbg_hqT = nc.dram_tensor("dbg_hqT", [128, 4096], BF16, kind="ExternalOutput").ap()
        dbg_stat = nc.dram_tensor("dbg_stat", [128, 64], F32, kind="ExternalOutput").ap()
        dbg_xmid = nc.dram_tensor("dbg_xmid", [OT, D], F32, kind="ExternalOutput").ap()
        dbg_y1T = nc.dram_tensor("dbg_y1T", [D, S], F32, kind="ExternalOutput").ap()
        dbg_KT = nc.dram_tensor("dbg_KT", [D, 512], F32, kind="ExternalOutput").ap()
        dbg_QT = nc.dram_tensor("dbg_QT", [D, 512], F32, kind="ExternalOutput").ap()
        dbg_P = nc.dram_tensor("dbg_P", [128, 1024], F32, kind="ExternalOutput").ap()
    wuq_s = nc.dram_tensor("wuq_s", [D, 4096], BF16, kind="Internal").ap()
    wdq_s = nc.dram_tensor("wdq_s", [4096, D], BF16, kind="Internal").ap()

    ARF = 30000
    ARB = 31000
    arf = nc.alloc_sbuf_tensor("arf", [128, ARF], F32)
    arb = nc.alloc_sbuf_tensor("arb", [128, ARB], BF16)
    ident = nc.alloc_sbuf_tensor("ident", [128, 128], F32)
    iof = nc.alloc_sbuf_tensor("iof", [128, 128], F32)
    iop = nc.alloc_sbuf_tensor("iop", [128, 1], F32)
    shiftm = nc.alloc_sbuf_tensor("shiftm", [64, 128], F32)
    onesrow = nc.alloc_sbuf_tensor("onesrow", [1, 128], F32)
    onescol = nc.alloc_sbuf_tensor("onescol", [128, 1], F32)
    bdg = nc.alloc_sbuf_tensor("bdg", [128, 2], F32)
    negmb = nc.alloc_sbuf_tensor("negmb", [128, 1], F32)
    mw127 = nc.alloc_sbuf_tensor("mw127", [128, 3], F32)
    dlt = nc.alloc_sbuf_tensor("dlt", [128, NM], F32)
    i32t = nc.alloc_sbuf_tensor("i32t", [128, 2048], I32)
    i32y = nc.alloc_sbuf_tensor("i32y", [128, 1024], I32)
    stat = nc.alloc_sbuf_tensor("stat", [128, 56], F32)
    ss = nc.alloc_sbuf_tensor("ss", [2, 104], F32)
    sm = nc.alloc_sbuf_tensor("sm", [1, 24], F32)
    identb = nc.alloc_sbuf_tensor("identb", [128, 128], BF16)
    epsb = nc.alloc_sbuf_tensor("epsb", [128, 1], F32)

    psA = nc.alloc_psum_tensor("psA", [128, 1024], F32)
    psA2 = nc.alloc_psum_tensor("psA2", [128, 1024], F32)
    psB = nc.alloc_psum_tensor("psB", [128, 1024], F32)
    psC = nc.alloc_psum_tensor("psC", [128, 1024], F32)

    LSEMS = ("Lxb0", "Lxb1", "Lqw0", "Lqw1", "Lyb0", "Lyb1", "Lwb", "Lkv0",
             "Lkv1", "Lqt", "Lwo", "Lxo0", "Lxo1", "Lxl0", "Lxl1", "Lwg0",
             "Lwg1", "Lwd0", "Lwd1", "Ldlt", "Lxr0", "Lxr1")
    SSEMS = ("Syst0", "Syst1", "Svst0", "Svst1", "Sqst0", "Sqst1", "Swqt0",
             "Swqt1", "Sxm0", "Sxm1", "Sxout", "Sy", "st")
    SEM = {}
    for n in ("ld", "te", "ve", "se", "ge", "sy") + LSEMS + SSEMS:
        SEM[n] = nc.alloc_semaphore(name=n)
    CNT = dict.fromkeys(SEM, 0)

    queues = {e: [] for e in ENGINES}

    def step(eng, waits, emitfn, incsem=None, amt=1):
        mark = None
        if incsem is not None:
            CNT[incsem] += amt
            mark = CNT[incsem]
        wl = [(s, v) for (s, v) in waits if v and v > 0]

        def fn(e, wl=wl, emitfn=emitfn, incsem=incsem, amt=amt):
            for s, v in wl:
                e.wait_ge(SEM[s], v)
            ins = emitfn(e)
            if incsem is not None and ins is not None:
                ins.then_inc(SEM[incsem], amt)
        queues[eng].append(fn)
        return mark

    def dld(e, sem, out, in_):
        e.dma_start(out=out, in_=in_).then_inc(SEM[sem], 16)

    def barrier():
        lwl = [(n, CNT[n]) for n in LSEMS + ("ld",)] + [("sy", CNT["sy"])]
        swl = [(n, CNT[n]) for n in SSEMS] + [("ge", CNT["ge"])]
        step("sync", lwl, lambda e: e.nop(), "sy")
        step("gpsimd", swl, lambda e: e.nop(), "ge")
        step("tensor", [("te", CNT["te"])], lambda e: e.nop(), "te")
        step("vector", [("ve", CNT["ve"])], lambda e: e.nop(), "ve")
        step("scalar", [("se", CNT["se"])], lambda e: e.nop(), "se")
        tg = {s: CNT[s] for s in ("sy", "ge", "te", "ve", "se")}
        own = {"sync": "sy", "gpsimd": "ge", "tensor": "te", "vector": "ve",
               "scalar": "se"}
        for eng, osem in own.items():
            wl = [(s, tg[s]) for s in tg if s != osem]
            step(eng, wl, lambda e: None, None)

    def flush(block):
        for eng in ENGINES:
            fns = queues[eng]
            queues[eng] = []
            if not fns:
                continue
            sec = getattr(block, eng)

            def body(e, fns=fns):
                for f in fns:
                    f(e)
            sec(body)

    blk = nc.Block()
    block = blk.__enter__()

    m_ld = step("sync", [], lambda e: e.dma_start(out=dlt[:, :], in_=deltas[:, :]),
                "Ldlt", 16)
    diffm = arf[:, 0:128]

    def p0_gp(e):
        e.iota(iof[:, :], pattern=[[1, 128]], base=0, channel_multiplier=0,
               allow_small_or_imprecise_dtypes=True)
        e.memset(onesrow[:, :], 1.0)
        e.memset(epsb[:, :], 1e-5)
        e.memset(onescol[:, :], 1.0)
        return e.iota(iop[:, :], pattern=[[0, 1]], base=0, channel_multiplier=1,
                      allow_small_or_imprecise_dtypes=True)
    m_ge = step("gpsimd", [], p0_gp, "ge")
    barrier()

    maskb = arb[:, 0:NM * 128]

    def p0_ve(e):
        e.tensor_scalar(ident[:, :], iof[:, :], iop[:, :], None, AluOpType.is_equal)
        e.drain()
        e.tensor_copy(identb[:, :], ident[:, :])
        e.tensor_scalar(stat[:, 0:1], iop[:, :], 64.0, None, AluOpType.add)
        e.drain()
        e.tensor_scalar(shiftm[:, :], iof[0:64, :], stat[0:64, 0:1], None,
                        AluOpType.is_equal)
        e.tensor_scalar(diffm, iof[:, :], iop[:, :], None, AluOpType.subtract)
        e.tensor_scalar(bdg[:, 0:1], iop[:, :], 64.0, None, AluOpType.is_lt)
        e.tensor_scalar(bdg[:, 1:2], iop[:, :], 64.0, None, AluOpType.is_ge)
        e.memset(psA[:, :], 0.0)
        e.memset(psA2[:, :], 0.0)
        for i in range(NM):
            e.tensor_scalar(maskb[:, i * 128:(i + 1) * 128], diffm,
                            dlt[:, i:i + 1], None, AluOpType.is_ge)
        return e.drain()
    step("vector", [("Ldlt", m_ld)], p0_ve, "ve")
    barrier()

    xb = [arf[:, 0:1024], arf[:, 1024:2048]]
    sqb = arf[:, 2048:3072]
    ybp = [arf[:, 3072:4096], arf[:, 4096:5120]]
    yst = [arf[:, 5120:6144], arf[:, 6144:7168]]
    wt = [arf[:, 7168:11264], arf[:, 11264:15360]]
    c1b = arf[:, 15360:19456]
    c2b = arf[:, 19456:23552]
    wqt = arb[:, 4096:8192]

    def emit_ln(src_ap, t, xbuf, ybuf, sb, prevp):
        mk = {}
        xbsem = "Lxb0" if t % 2 == 0 else "Lxb1"
        mk["ld"] = step("sync", [("ve", prevp.get("y1", 0))],
                        lambda e: e.dma_start(out=xbuf[:, :],
                                              in_=src_ap[t * 128:(t + 1) * 128, :]),
                        xbsem, 16)

        def acts(e):
            e.activation(sqb[:, :], xbuf[:, :], AF.Square,
                         accum_out=stat[:, sb:sb + 1])
            e.activation(sqb[:, :], xbuf[:, :], AF.Copy,
                         accum_out=stat[:, sb + 1:sb + 2])
            return e.drain()
        mk["acc"] = step("scalar", [(xbsem, mk["ld"]), ("ve", prevp.get("var", 0))],
                         acts, "se")

        def stats(e):
            e.tensor_scalar(stat[:, sb + 2:sb + 3], stat[:, sb + 1:sb + 2],
                            1.0 / 1024, None, AluOpType.mult)
            e.drain()
            e.tensor_tensor(stat[:, sb + 4:sb + 5], stat[:, sb + 2:sb + 3],
                            stat[:, sb + 2:sb + 3], AluOpType.mult)
            e.drain()
            e.scalar_tensor_tensor(stat[:, sb + 6:sb + 7],
                                       stat[:, sb:sb + 1], 1.0 / 1024,
                                       stat[:, sb + 4:sb + 5],
                                       AluOpType.mult, AluOpType.subtract)
            return e.drain()
        mk["var"] = step("vector", [("se", mk["acc"])], stats, "ve")
        def sqrtf(e):
            e.activation(stat[:, sb + 7:sb + 8], stat[:, sb + 6:sb + 7], AF.Sqrt,
                         bias=epsb[:, :])
            return e.drain()
        mk["sqrt"] = step("scalar", [("ve", mk["var"]), ("ve", prevp.get("y1", 0))],
                          sqrtf, "se")

        def norm(e):
            e.reciprocal(stat[:, sb + 8:sb + 9], stat[:, sb + 7:sb + 8])
            e.drain()
            e.tensor_scalar(ybuf[:, :], xbuf[:, :], stat[:, sb + 2:sb + 3],
                            stat[:, sb + 8:sb + 9],
                            AluOpType.subtract, AluOpType.mult)
            return e.drain()
        mk["y1"] = step("vector", [("se", mk["sqrt"]), ("te", prevp.get("tp", 0))],
                        norm, "ve")

        ps = psB if t % 2 == 0 else psC

        def tps(e):
            last = None
            for c in range(8):
                last = e.matmul(ps[:, c * 128:(c + 1) * 128],
                                ybuf[:, c * 128:(c + 1) * 128], ident[:, :],
                                is_transpose=True, start=True, stop=True)
            return last
        mk["tp"] = step("tensor", [("ve", mk["y1"]), ("ve", prevp.get("ev", 0))],
                        tps, "te")
        ystb = yst[t % 2]
        def evf(e):
            e.tensor_copy(ystb[:, :], ps[:, :])
            return e.drain()
        ystsem = "Syst0" if t % 2 == 0 else "Syst1"
        mk["ev"] = step("vector", [("te", mk["tp"]), (ystsem, prevp.get("st", 0))],
                        evf, "ve")
        return mk, ystb

    def ln_stream(src_ap, ntiles, dsts, prev=None):
        if prev is None:
            prev = [{}, {}]
        for t in range(ntiles):
            sb = ST_LN + (t % 2) * 13
            mk, ystb = emit_ln(src_ap, t, xb[t % 2], ybp[t % 2], sb, prev[t % 2])

            stsem = "Syst0" if t % 2 == 0 else "Syst1"

            def store(e, t=t, ystb=ystb, stsem=stsem):
                for c in range(8):
                    dld(e, stsem, dsts[c * 128:(c + 1) * 128, t * 128:(t + 1) * 128],
                        ystb[:, c * 128:(c + 1) * 128])
                return None
            mk["st"] = step("gpsimd", [("ve", mk["ev"])], store, stsem, 16 * 8)
            prev[t % 2] = mk
        return prev

    _lnp = ln_stream(x_full, 32, y1T_s)
    ln_stream(x_own, 16, y1To_s, _lnp)


    barrier()
    flush(block)

    wbuf = arf[:, 0:8192]
    ybuf = [arf[:, 8192:12288], arf[:, 12288:16384]]
    qst = [arf[:, 16384:16896], arf[:, 16896:17408]]
    sqt = [arf[:, 17408:17920], arf[:, 17920:18432]]
    vst = [arf[:, 18432:19472], arf[:, 19472:20512]]
    wtq = [arf[:, 20512:22560], arf[:, 22560:24608]]
    c1b = arf[:, 24608:26656]
    c2b = arf[:, 26656:28704]
    wqt2 = [arb[:, 4096:6144], arb[:, 6144:8192]]

    nidx = [0]
    pb_prev = {}
    pc_prev = {}
    st_prev = {}
    bd_prev = {}
    vb_prev = {}
    vst_st = {}

    def quant_units():
        carry_qz = 0
        carry_st = {}
        for (src_ap, ntiles, tile_w, dsts, mwcol) in [
                (wgT_in, 8, 4096, wgq_s, 0),
                (wuT_in, 8, 4096, wuq_s, 1),
                (wdT_in, 32, 1024, wdq_s, 2)]:
            cw = min(tile_w, 2048)
            nch = tile_w // cw
            step("vector", [], lambda e: e.memset(stat[:, ST_WQ:ST_WQ + 1], 0.0),
                 "ve")
            prev_red = [carry_qz, carry_qz]
            ui = 0
            for t in range(ntiles):
                for c in range(nch):
                    b = ui % 2
                    qsem = "Lqw0" if b == 0 else "Lqw1"
                    mld = step("scalar", [("ve", prev_red[b])],
                               lambda e, t=t, c=c, b=b, cw=cw, src_ap=src_ap: e.dma_start(
                                   out=wtq[b][:, 0:cw],
                                   in_=src_ap[t * 128:(t + 1) * 128,
                                              c * cw:(c + 1) * cw]), qsem, 16)

                    def red(e, b=b, cw=cw):
                        e.reduce_sum(stat[:, ST_WQ + 1:ST_WQ + 2], wtq[b][:, 0:cw],
                                     axis=AXX, apply_absolute_value=True)
                        e.drain()
                        e.tensor_tensor(stat[:, ST_WQ:ST_WQ + 1],
                                        stat[:, ST_WQ:ST_WQ + 1],
                                        stat[:, ST_WQ + 1:ST_WQ + 2],
                                        AluOpType.add)
                        return e.drain()
                    prev_red[b] = step("vector", [(qsem, mld)], red, "ve")
                    ui += 1
                    yield None
            mA = step("tensor", [("ve", max(prev_red)), ("ve", pc_prev.get(0, 0))],
                      lambda e: e.matmul(psC[0:1, 0:1], stat[:, ST_WQ:ST_WQ + 1],
                                         onescol[:, :], start=True, stop=True),
                      "te")

            def mk_m(e):
                e.tensor_scalar(sm[:, 8:9], psC[0:1, 0:1], 1.0 / 4194304, None,
                                AluOpType.mult)
                e.drain()
                e.tensor_scalar(sm[:, 9:10], sm[0:1, 8:9], 1e-5, None,
                                AluOpType.max)
                return e.drain()
            mB = step("vector", [("te", mA)], mk_m, "ve")
            pc_prev[0] = mB
            mC = step("tensor", [("ve", mB), ("ve", pc_prev.get(1, 0))],
                      lambda e: e.matmul(psC[:, 512:513], onesrow[0:1, :],
                                         sm[0:1, 9:10], start=True, stop=True),
                      "te")

            def thr(e, mwcol=mwcol):
                e.tensor_copy(stat[:, ST_WQ + 2:ST_WQ + 3], psC[:, 512:513])
                e.drain()
                e.tensor_scalar(stat[:, ST_WQ + 3:ST_WQ + 4],
                                stat[:, ST_WQ + 2:ST_WQ + 3], 0.5, None,
                                AluOpType.mult)
                e.drain()
                e.tensor_scalar(stat[:, ST_WQ + 4:ST_WQ + 5],
                                stat[:, ST_WQ + 3:ST_WQ + 4], -1.0, None,
                                AluOpType.mult)
                return e.tensor_scalar(mw127[:, mwcol:mwcol + 1],
                                       stat[:, ST_WQ + 2:ST_WQ + 3],
                                       1.0 / 127, None, AluOpType.mult)
            mD = step("vector", [("te", mC)], thr, "ve")
            pc_prev[1] = mD
            yield None
            prev_q = [mD, mD]
            prev_st = {}
            ui = 0
            for t in range(ntiles):
                for c in range(nch):
                    b = ui % 2
                    qsem = "Lqw0" if b == 0 else "Lqw1"
                    mld = step("scalar", [("ve", prev_q[b])],
                               lambda e, t=t, c=c, b=b, cw=cw, src_ap=src_ap: e.dma_start(
                                   out=wtq[b][:, 0:cw],
                                   in_=src_ap[t * 128:(t + 1) * 128,
                                              c * cw:(c + 1) * cw]), qsem, 16)

                    def qz(e, b=b, cw=cw):
                        e.tensor_scalar(c1b[:, 0:cw], wtq[b][:, 0:cw],
                                        stat[:, ST_WQ + 3:ST_WQ + 4], None,
                                        AluOpType.is_gt)
                        e.tensor_scalar(c2b[:, 0:cw], wtq[b][:, 0:cw],
                                        stat[:, ST_WQ + 4:ST_WQ + 5], None,
                                        AluOpType.is_lt)
                        e.drain()
                        e.tensor_tensor(wqt2[b][:, 0:cw], c1b[:, 0:cw],
                                        c2b[:, 0:cw], AluOpType.subtract)
                        return e.drain()
                    wsem = "Swqt0" if b == 0 else "Swqt1"
                    wl = [(qsem, mld), (wsem, prev_st.get(b, carry_st.get(b, 0)))]
                    mq_ = step("vector", wl, qz, "ve")
                    prev_q[b] = mq_
                    stp = step("gpsimd", [("ve", mq_)],
                               lambda e, t=t, c=c, b=b, cw=cw, dsts=dsts: e.dma_start(
                                   out=dsts[t * 128:(t + 1) * 128,
                                            c * cw:(c + 1) * cw],
                                   in_=wqt2[b][:, 0:cw]), wsem, 16)
                    prev_st[b] = stp
                    carry_st[b] = stp
                    ui += 1
                    yield None
            carry_qz = max(prev_q)

    QG = quant_units()
    qg_done = [False]

    def pump(n):
        for _ in range(n):
            if qg_done[0]:
                return
            try:
                next(QG)
            except StopIteration:
                qg_done[0] = True
                return

    def loadW(dstb, src, wait_te):
        def f(e):
            for c in range(8):
                dld(e, "Lwb", dstb[:, c * 1024:(c + 1) * 1024],
                    src[c * 128:(c + 1) * 128, :])
            return None
        return step("sync", [("te", wait_te)], f, "Lwb", 16 * 8)

    def vst_ones(e):
        for t in range(2):
            vv = vst[t].rearrange("p (h d) -> p h d", d=65)
            e.memset(vv[:, :, 64:65], 1.0)
        return e.drain()
    m_ones = step("vector", [], vst_ones, "ve")

    def proj_block(mk_deps, wbufx, ybufg, db, store_fn):
        par = db % 2
        pslice = psB[:, par * 512:(par + 1) * 512]
        wl = list(mk_deps) + [("se", pb_prev.get(par, 0))]

        def mms(e):
            last = None
            for c in range(8):
                last = e.matmul(pslice,
                                wbufx[:, c * 1024 + (db % 8) * 128:c * 1024 + (db % 8) * 128 + 128],
                                ybufg[:, c * 512:(c + 1) * 512],
                                start=(c == 0), stop=(c == 7))
            return last
        mk_mm = step("tensor", wl, mms, "te")

        def evac(e):
            e.copy(qst[par][:, :], pslice)
            e.drain()
            return e.activation(sqt[par][:, :], qst[par][:, :], AF.Square)
        qsem = "Sqst0" if par == 0 else "Sqst1"
        mk_ev = step("scalar", [("te", mk_mm), (qsem, st_prev.get(par, 0)),
                                ("te", bd_prev.get(par, 0))], evac, "se")
        pb_prev[par] = mk_ev
        st_prev[par] = step("gpsimd", [("se", mk_ev)],
                            lambda e: store_fn(e, qst[par]), qsem, 16)
        cpar = nidx[0] % 2
        cslice = psC[0:2, cpar * 512:cpar * 512 + 512]
        mk_bd = step("tensor", [("se", mk_ev), ("ve", pc_prev.get(cpar, 0))],
                     lambda e: e.matmul(cslice, bdg[:, :], sqt[par][:, :],
                                        start=True, stop=True), "te")
        bd_prev[par] = mk_bd
        col = nidx[0]
        pc_prev[cpar] = step("vector", [("te", mk_bd)],
                             lambda e: e.reduce_max(ss[0:2, col:col + 1], cslice,
                                                    axis=AXX), "ve")
        nidx[0] += 1
        return mk_mm

    mW = loadW(wbuf, wk_in, 0)
    grp_last = {}
    for g in range(8):
        ybsem = "Lyb0" if g % 2 == 0 else "Lyb1"

        def ldy(e, g=g, ybsem=ybsem):
            for c in range(8):
                dld(e, ybsem, ybuf[g % 2][:, c * 512:(c + 1) * 512],
                    y1T_s[c * 128:(c + 1) * 128, g * 512:(g + 1) * 512])
            return None
        mldy = step("sync", [("te", grp_last.get(g - 2, 0))], ldy, ybsem, 16 * 8)
        for db in range(8):
            proj_block([(ybsem, mldy), ("Lwb", mW)], wbuf, ybuf[g % 2], db,
                       lambda e, buf, g=g, db=db: e.dma_start(
                           out=KT_s[db * 128:(db + 1) * 128, g * 512:(g + 1) * 512],
                           in_=buf[:, :]))
        grp_last[g] = CNT["te"]
        pump(7)

    mW = loadW(wbuf, wv_in, CNT["te"])
    grp_last = {}
    for g in range(8):
        ybsem = "Lyb0" if g % 2 == 0 else "Lyb1"

        def ldy(e, g=g, ybsem=ybsem):
            for c in range(8):
                dld(e, ybsem, ybuf[g % 2][:, c * 512:(c + 1) * 512],
                    y1T_s[c * 128:(c + 1) * 128, g * 512:(g + 1) * 512])
            return None
        mldy = step("sync", [("te", grp_last.get(g - 2, 0))], ldy, ybsem, 16 * 8)
        for tc in range(4):
            for sl in range(2):
                idx = tc * 2 + sl
                pv = (psA if idx % 2 == 0 else psA2)[:, 0:512]
                wl = [(ybsem, mldy), ("Lwb", mW), ("ve", vb_prev.get(idx % 2, 0))]

                def vmms(e, g=g, tc=tc, sl=sl, pv=pv):
                    last = None
                    for c in range(8):
                        last = e.matmul(
                            pv,
                            ybuf[g % 2][:, c * 512 + tc * 128:c * 512 + tc * 128 + 128],
                            wbuf[:, c * 1024 + sl * 512:c * 1024 + (sl + 1) * 512],
                            start=(c == 0), stop=(c == 7))
                    return last
                mk_vm = step("tensor", wl, vmms, "te")
                vsem = "Svst0" if tc % 2 == 0 else "Svst1"
                wl2 = [("te", mk_vm), ("ve", m_ones),
                       (vsem, vst_st.get(tc % 2, 0) if sl == 0 else 0)]

                def vev(e, tc=tc, sl=sl, pv=pv):
                    vv = vst[tc % 2].rearrange("p (h d) -> p h d", d=65)
                    pvv = pv.rearrange("p (h d) -> p h d", d=64)
                    e.tensor_copy(vv[:, sl * 8:(sl + 1) * 8, 0:64], pvv)
                    return e.drain()
                mk_ve = step("vector", wl2, vev, "ve")
                vb_prev[idx % 2] = mk_ve
                if sl == 1:
                    vst_st[tc % 2] = step(
                        "gpsimd", [("ve", mk_ve)],
                        lambda e, g=g, tc=tc: e.dma_start(
                            out=V_s[g * 4 + tc, :, :], in_=vst[tc % 2][:, :]),
                        vsem, 16)
        grp_last[g] = CNT["te"]
        pump(7)

    mW = loadW(wbuf, wq_in, CNT["te"])
    grp_last = {}
    for g in range(4):
        ybsem = "Lyb0" if g % 2 == 0 else "Lyb1"

        def ldy(e, g=g, ybsem=ybsem):
            for c in range(8):
                dld(e, ybsem, ybuf[g % 2][:, c * 512:(c + 1) * 512],
                    y1To_s[c * 128:(c + 1) * 128, g * 512:(g + 1) * 512])
            return None
        mldy = step("sync", [("te", grp_last.get(g - 2, 0))], ldy, ybsem, 16 * 8)
        for db in range(8):
            proj_block([(ybsem, mldy), ("Lwb", mW)], wbuf, ybuf[g % 2], db,
                       lambda e, buf, g=g, db=db: e.dma_start(
                           out=QT_s[db * 128:(db + 1) * 128, g * 512:(g + 1) * 512],
                           in_=buf[:, :]))
        grp_last[g] = CNT["te"]
        pump(8)
    pump(1000)

    def mred(e):
        e.reduce_max(ss[0:2, 100:101], ss[0:2, 0:64], axis=AXX)
        e.reduce_max(ss[0:2, 101:102], ss[0:2, 64:96], axis=AXX)
        return e.drain()
    mk_nr = step("vector", [("ve", CNT["ve"])], mred, "ve")
    mk_tp = step("tensor", [("ve", mk_nr)],
                 lambda e: e.matmul(psC[0:2, 0:2], ss[0:2, 100:102],
                                    ident[0:2, 0:2], is_transpose=True,
                                    start=True, stop=True), "te")

    def mprod(e):
        e.tensor_copy(sm[:, 10:12], psC[0:1, 0:2])
        e.drain()
        e.tensor_tensor(sm[:, 12:13], sm[0:1, 10:11], sm[0:1, 11:12],
                        AluOpType.mult)
        return e.drain()
    mk_pr = step("vector", [("te", mk_tp)], mprod, "ve")
    def msqf(e):
        e.activation(sm[:, 13:14], sm[0:1, 12:13], AF.Sqrt, scale=1.0 / 64)
        return e.drain()
    mk_sq = step("scalar", [("ve", mk_pr)], msqf, "se")

    def mneg(e):
        e.tensor_scalar(sm[:, 14:15], sm[0:1, 13:14], 1.01, 1e-6,
                        AluOpType.mult, AluOpType.add)
        e.drain()
        e.tensor_scalar(sm[:, 15:16], sm[0:1, 14:15], -1.0, None,
                        AluOpType.mult)
        return e.drain()
    mk_ng = step("vector", [("se", mk_sq)], mneg, "ve")
    mk_bc = step("tensor", [("ve", mk_ng)],
                 lambda e: e.matmul(psC[:, 0:1], onesrow[0:1, :], sm[0:1, 15:16],
                                    start=True, stop=True), "te")
    def ngf(e):
        e.tensor_copy(negmb[:, :], psC[:, 0:1])
        return e.drain()
    step("vector", [("te", mk_bc)], ngf, "ve")
    barrier()
    flush(block)

    qtb = arf[:, 0:4096]
    ktb = [arf[:, 4096:6144], arf[:, 6144:8192]]
    vbb = [arf[:, 8192:10272], arf[:, 10272:12352]]
    Pb = [arf[:, 12352:13376], arf[:, 13376:14400]]
    srp = [arf[:, 14400:14912], arf[:, 14912:15424]]
    ACC = 15424
    accs = [arf[:, ACC + h * 512:ACC + (h + 1) * 512] for h in range(16)]
    xob = [arf[:, 23616:24640], arf[:, 24640:25664]]
    xmst = [arf[:, 25664:26688], arf[:, 26688:27712]]
    wob = arf[:, 0:8192]
    attnT = arf[:, 8192:12288]

    o_done_prev = 0
    asm_done_prev = 0

    for q in range(NQ):
        def ldq(e, q=q):
            for c in range(8):
                dld(e, "Lqt", qtb[:, c * 512:(c + 1) * 512],
                    QT_s[c * 128:(c + 1) * 128, q * 512:(q + 1) * 512])
            return None
        m_qt = step("sync", [("te", o_done_prev)], ldq, "Lqt", 16 * 8)

        def zacc(e):
            for h in range(16):
                e.memset(accs[h][0:65, :], 0.0)
            return e.nop()
        m_z = step("vector", [("te", asm_done_prev)], zacc, "ve")

        exp_mk = {}
        pv_mk = {}
        msk_mk = {}
        add_mk = {}
        sc_mk = {}
        kg_ld = {}

        def prevmk(dd, kg, h, delta=2):
            if (kg, h - delta) in dd:
                return dd[(kg, h - delta)]
            return dd.get((kg - 1, h + 16 - delta), 0)

        for kg in range(KGQ[q]):
            kvsem = "Lkv0" if kg % 2 == 0 else "Lkv1"

            def ldkv(e, q=q, kg=kg, kvsem=kvsem):
                for c in range(8):
                    dld(e, kvsem, ktb[kg % 2][:, c * 256:(c + 1) * 256],
                        KT_s[c * 128:(c + 1) * 128, kg * 256:(kg + 1) * 256])
                for i in range(2):
                    dld(e, kvsem, vbb[kg % 2][:, i * 1040:(i + 1) * 1040],
                        V_s[kg * 2 + i, :, :])
                return None
            if kg >= 2:
                wl = [("te", pv_mk[(kg - 2, 15)])]
            else:
                wl = [("te", o_done_prev)]
            kg_ld[kg] = step("sync", wl, ldkv, kvsem, 16 * 10)

            def emit_scores(h):
                psS = psA if h % 2 == 0 else psA2
                kcs = [2 * kg, 2 * kg + 1]
                sfx = [_suffix(q, kc) for kc in kcs]

                def scores(e, kg=kg, h=h, kcs=kcs, sfx=sfx, psS=psS):
                    last = None
                    for i in range(2):
                        s = sfx[i]
                        last = e.matmul(
                            psS[:, i * 512 + s * 128:(i + 1) * 512],
                            ktb[kg % 2][(h % 2) * 64:(h % 2) * 64 + 64,
                                        (h // 2) * 256 + i * 128:(h // 2) * 256 + (i + 1) * 128],
                            qtb[(h % 2) * 64:(h % 2) * 64 + 64,
                                (h // 2) * 512 + s * 128:(h // 2) * 512 + 512],
                            start=True, stop=True)
                    return last
                kvsem = "Lkv0" if kg % 2 == 0 else "Lkv1"
                wl = [(kvsem, kg_ld[kg]), ("Lqt", m_qt),
                      ("se", prevmk(exp_mk, kg, h))]
                sc_mk[(kg, h)] = step("tensor", wl, scores, "te")

                def expf(e, h=h, psS=psS):
                    e.activation(Pb[h % 2][:, :], psS[:, :], AF.Exp,
                                 bias=negmb[:, :], scale=0.125)
                    return e.drain()
                wl = [("te", sc_mk[(kg, h)]), ("te", prevmk(pv_mk, kg, h))]
                exp_mk[(kg, h)] = step("scalar", wl, expf, "se")

                mts = []
                for i, kc in enumerate(kcs):
                    for r in range(sfx[i], 4):
                        if (q, kc, r) in MIDX:
                            mts.append((i, r, MIDX[(q, kc, r)]))
                if mts:
                    def maskf(e, h=h, mts=mts):
                        last = None
                        for i, r, idx in mts:
                            sl = Pb[h % 2][:, i * 512 + r * 128:i * 512 + (r + 1) * 128]
                            last = e.tensor_tensor(
                                sl, sl, maskb[:, idx * 128:(idx + 1) * 128],
                                AluOpType.mult)
                        return e.drain()
                    msk_mk[(kg, h)] = step("vector", [("se", exp_mk[(kg, h)])],
                                           maskf, "ve")

            def emit_pv(h):
                kcs = [2 * kg, 2 * kg + 1]
                sfx = [_suffix(q, kc) for kc in kcs]

                def pvf(e, kg=kg, h=h, sfx=sfx):
                    last = None
                    for i in range(2):
                        s = sfx[i]
                        last = e.matmul(
                            psB[0:65, (h % 2) * 512 + s * 128:(h % 2) * 512 + 512],
                            vbb[kg % 2][:, i * 1040 + h * 65:i * 1040 + h * 65 + 65],
                            Pb[h % 2][:, i * 512 + s * 128:(i + 1) * 512],
                            start=(i == 0), stop=(i == 1))
                    return last
                wl = [("se", exp_mk[(kg, h)]), ("ve", prevmk(add_mk, kg, h)),
                      ("ve", m_z)]
                if (kg, h) in msk_mk:
                    wl.append(("ve", msk_mk[(kg, h)]))
                pv_mk[(kg, h)] = step("tensor", wl, pvf, "te")

                sg = sfx[0]

                def addf(e, kg=kg, h=h, sg=sg):
                    sl = accs[h][0:65, sg * 128:512]
                    return e.tensor_tensor(
                        sl, sl,
                        psB[0:65, (h % 2) * 512 + sg * 128:(h % 2) * 512 + 512],
                        AluOpType.add)
                add_mk[(kg, h)] = step("vector", [("te", pv_mk[(kg, h)])], addf,
                                       "ve")

            for h in range(17):
                if h < 16:
                    emit_scores(h)
                if h >= 1:
                    emit_pv(h - 1)

        if "p3kg" in SER:
            barrier()
        last_kg = KGQ[q] - 1

        def ldwo(e):
            for c in range(8):
                dld(e, "Lwo", wob[:, c * 1024:(c + 1) * 1024],
                    wo_in[c * 128:(c + 1) * 128, :])
            return None
        m_wo = step("sync", [("te", pv_mk[(last_kg, 15)])], ldwo, "Lwo", 16 * 8)

        add_last = add_mk[(last_kg, 15)]
        normPs = [psB[:, 0:512], psB[:, 512:1024], psA[:, 0:512]]
        asmPs = [psC[:, 0:512], psC[:, 512:1024], psA2[:, 0:512]]
        asm_mk = {}
        bc_mk = {}
        rcp_mk = {}
        at_mk = {}
        for j in range(8):
            def rcpf(e, j=j):
                e.reciprocal(arf[0:1, 12352 + (j % 3) * 1024:
                                  12352 + (j % 3) * 1024 + 1024],
                             arf[64:65, ACC + 2 * j * 512:
                                 ACC + 2 * j * 512 + 1024])
                return e.drain()
            rcp_mk[j] = step("vector", [("ve", add_last),
                                        ("te", bc_mk.get(j - 3, 0))], rcpf, "ve")

            def bcf(e, j=j):
                base = 12352 + (j % 3) * 1024
                ps = normPs[j % 3]
                last = None
                for i in range(2):
                    last = e.matmul(
                        ps[i * 64:(i + 1) * 64, :],
                        onesrow[0:1, 0:64],
                        arf[0:1, base + i * 512:base + (i + 1) * 512],
                        start=True, stop=True)
                return last
            bc_mk[j] = step("tensor",
                            [("ve", rcp_mk[j]), ("ve", at_mk.get(j - 3, 0))],
                            bcf, "te")

            def asmf(e, j=j):
                ps = asmPs[j % 3]
                e.matmul(ps, ident[0:64, :], accs[2 * j][0:64, :],
                         start=True, stop=False)
                return e.matmul(ps, shiftm[:, :], accs[2 * j + 1][0:64, :],
                                start=False, stop=True)
            asm_mk[j] = step("tensor", [("ve", add_last)], asmf, "te")

            def atf1(e, j=j):
                e.copy(attnT[:, j * 512:(j + 1) * 512], asmPs[j % 3])
                return e.drain()
            at_se = step("scalar", [("te", asm_mk[j])], atf1, "se")

            def atf2(e, j=j):
                sl = attnT[:, j * 512:(j + 1) * 512]
                e.tensor_tensor(sl, sl, normPs[j % 3], AluOpType.mult)
                return e.drain()
            at_mk[j] = step("vector", [("se", at_se), ("te", bc_mk[j])],
                            atf2, "ve")

        xev_mk = {}
        o_mk = {}
        xst_mk = {}
        for tsub in range(4):
            xosem = "Lxo0" if tsub % 2 == 0 else "Lxo1"
            xo_mk = step("scalar", [("ve", xev_mk.get((tsub - 2, 1), 0))],
                         lambda e, q=q, tsub=tsub: e.dma_start(
                             out=xob[tsub % 2][:, :],
                             in_=x_own[(q * 4 + tsub) * 128:(q * 4 + tsub + 1) * 128, :]),
                         xosem, 16)
            for osl in range(2):
                pidx = (tsub * 2 + osl) % 2

                def omms(e, tsub=tsub, osl=osl, pidx=pidx):
                    last = None
                    for j in range(8):
                        last = e.matmul(
                            psC[:, pidx * 512:pidx * 512 + 512],
                            attnT[:, j * 512 + tsub * 128:j * 512 + tsub * 128 + 128],
                            wob[:, j * 1024 + osl * 512:j * 1024 + (osl + 1) * 512],
                            start=(j == 0), stop=(j == 7))
                    return last
                wl = [("ve", at_mk[7]), ("Lwo", m_wo),
                      ("ve", xev_mk.get((tsub - 1, osl), 0))]
                o_mk[(tsub, osl)] = step("tensor", wl, omms, "te")

                def xevf(e, tsub=tsub, osl=osl, pidx=pidx):
                    e.tensor_tensor(
                        xmst[tsub % 2][:, osl * 512:(osl + 1) * 512],
                        psC[:, pidx * 512:pidx * 512 + 512],
                        xob[tsub % 2][:, osl * 512:(osl + 1) * 512],
                        AluOpType.add)
                    return e.drain()
                xmsem = "Sxm0" if tsub % 2 == 0 else "Sxm1"
                wl = [("te", o_mk[(tsub, osl)]), (xosem, xo_mk),
                      (xmsem, xst_mk.get(tsub - 2, 0))]
                xev_mk[(tsub, osl)] = step("vector", wl, xevf, "ve")
            xmsem = "Sxm0" if tsub % 2 == 0 else "Sxm1"
            xst_mk[tsub] = step("gpsimd",
                                [("ve", xev_mk[(tsub, 0)]),
                                 ("ve", xev_mk[(tsub, 1)])],
                                lambda e, q=q, tsub=tsub: e.dma_start(
                                    out=xmid_s[(q * 4 + tsub) * 128:(q * 4 + tsub + 1) * 128, :],
                                    in_=xmst[tsub % 2][:, :]), xmsem, 16)
        if "p3tail" in SER:
            barrier()
        o_done_prev = o_mk[(3, 1)]
        asm_done_prev = asm_mk[7]
        flush(block)

    barrier()

    gh = [arf[:, i * 4096:(i + 1) * 4096] for i in range(4)]
    dumb = arf[:, 22528:23552]
    corr1 = arf[:, 23552:24576]
    corr2 = arf[:, 24576:25600]
    y2b = arf[:, 16384:17408]
    xmld = [arf[:, 17408:18432], arf[:, 18432:19456]]
    xout = arf[:, 19456:20480]
    xmr = [arf[:, 20480:21504], arf[:, 21504:22528]]
    y2qT = arb[:, 0:4096]
    wgb = [arb[:, 4096:8192], arb[:, 8192:12288]]
    wub = [arb[:, 12288:16384], arb[:, 16384:20480]]
    hqT = arb[:, 4096:20480]
    wdb = [arb[:, 20480:24576], arb[:, 24576:28672]]
    hqs = arb[:, 28672:29184]
    y2qb = arb[:, 29184:30208]

    if DEBUG2:
        def dbgdump2(e):
            e.dma_start(out=dbg_xmid2, in_=xmid_s[:, :]).then_inc(SEM["st"], 16)
            e.dma_start(out=dbg_wgq, in_=wgq_s[:, :]).then_inc(SEM["st"], 16)
            e.dma_start(out=dbg_wuq, in_=wuq_s[:, :]).then_inc(SEM["st"], 16)
            e.dma_start(out=dbg_wdq, in_=wdq_s[:, :]).then_inc(SEM["st"], 16)
            return None
        step("gpsimd", [], dbgdump2, "st", 64)
    if DEBUG:
        def dbgdump(e):
            e.dma_start(out=dbg_stat[:, 0:56], in_=stat[:, :]).then_inc(SEM["st"], 16)
            e.dma_start(out=dbg_stat[:, 56:57], in_=negmb[:, :], allow_slow_non_contiguous=True).then_inc(SEM["st"], 16)
            e.dma_start(out=dbg_stat[:, 57:60], in_=mw127[:, :], allow_slow_non_contiguous=True).then_inc(SEM["st"], 16)
            e.dma_start(out=dbg_stat[0:2, 60:62], in_=ss[:, 100:102], allow_slow_non_contiguous=True).then_inc(SEM["st"], 16)
            e.dma_start(out=dbg_stat[0:1, 62:63], in_=sm[:, 15:16], allow_slow_non_contiguous=True).then_inc(SEM["st"], 16)
            e.dma_start(out=dbg_xmid, in_=xmid_s[:, :]).then_inc(SEM["st"], 16)
            e.dma_start(out=dbg_y1T, in_=y1T_s[:, :]).then_inc(SEM["st"], 16)
            e.dma_start(out=dbg_KT, in_=KT_s[:, 0:512]).then_inc(SEM["st"], 16)
            e.dma_start(out=dbg_QT, in_=QT_s[:, 0:512]).then_inc(SEM["st"], 16)
            e.dma_start(out=dbg_P, in_=Pb[0][:, :]).then_inc(SEM["st"], 16)
            return None
        step("gpsimd", [], dbgdump, "st", 160)
    dn_ev_prev = 0
    dn_st_prev = 0
    dn23_prev = 0
    ln_prev = {}
    for tt in range(4):
        dprev = dn_ev_prev
        sprev = dn_st_prev
        y2tp_mk = {}
        ln_mk = {}
        tp_mk = {}
        for sub in range(4):
            sb = ST_LN + (sub % 2) * 13
            xlsem = "Lxl0" if sub % 2 == 0 else "Lxl1"
            mld = step("scalar", [("ve", ln_mk.get(sub - 2, ln_prev.get(sub + 2, 0)))],
                       lambda e, tt=tt, sub=sub: e.dma_start(
                           out=xmld[sub % 2][:, :],
                           in_=xmid_s[(tt * 4 + sub) * 128:(tt * 4 + sub + 1) * 128, :]),
                       xlsem, 16)

            def acts(e, sub=sub, sb=sb):
                e.activation(dumb[:, :], xmld[sub % 2][:, :], AF.Square,
                             accum_out=stat[:, sb:sb + 1])
                e.activation(dumb[:, :], xmld[sub % 2][:, :], AF.Copy,
                             accum_out=stat[:, sb + 1:sb + 2])
                return e.drain()
            macc = step("scalar", [(xlsem, mld),
                                   ("ve", ln_mk.get(sub - 2, ln_prev.get(sub + 2, 0)))],
                        acts, "se")

            def stats(e, sb=sb):
                e.tensor_scalar(stat[:, sb + 2:sb + 3], stat[:, sb + 1:sb + 2],
                                1.0 / 1024, None, AluOpType.mult)
                e.drain()
                e.tensor_tensor(stat[:, sb + 4:sb + 5], stat[:, sb + 2:sb + 3],
                                stat[:, sb + 2:sb + 3], AluOpType.mult)
                e.drain()
                e.scalar_tensor_tensor(stat[:, sb + 6:sb + 7],
                                       stat[:, sb:sb + 1], 1.0 / 1024,
                                       stat[:, sb + 4:sb + 5],
                                       AluOpType.mult, AluOpType.subtract)
                return e.drain()
            mvar = step("vector", [("se", macc)], stats, "ve")
            def msqf2(e, sb=sb):
                e.activation(stat[:, sb + 7:sb + 8], stat[:, sb + 6:sb + 7],
                             AF.Sqrt, bias=epsb[:, :])
                return e.drain()
            msq = step("scalar", [("ve", mvar)], msqf2, "se")

            def nq(e, sub=sub, sb=sb):
                e.reciprocal(stat[:, sb + 8:sb + 9], stat[:, sb + 7:sb + 8])
                e.drain()
                e.tensor_scalar(y2b[:, :], xmld[sub % 2][:, :],
                                stat[:, sb + 2:sb + 3], stat[:, sb + 8:sb + 9],
                                AluOpType.subtract, AluOpType.mult)
                e.reduce_max(stat[:, ST_YQ:ST_YQ + 1], y2b[:, :], axis=AXX,
                             apply_absolute_value=True)
                e.drain()
                e.tensor_scalar(stat[:, ST_YQ + 1:ST_YQ + 2],
                                stat[:, ST_YQ:ST_YQ + 1], 1e-5, None, AluOpType.max)
                e.drain()
                e.reciprocal(stat[:, ST_YQ + 2:ST_YQ + 3],
                             stat[:, ST_YQ + 1:ST_YQ + 2])
                e.drain()
                e.tensor_scalar(stat[:, ST_YQ + 3:ST_YQ + 4],
                                stat[:, ST_YQ + 2:ST_YQ + 3], 127.0, None,
                                AluOpType.mult)
                e.tensor_scalar(stat[:, ST_CTG + sub:ST_CTG + sub + 1],
                                stat[:, ST_YQ + 1:ST_YQ + 2], mw127[:, 0:1], None,
                                AluOpType.mult)
                e.tensor_scalar(stat[:, ST_CTU + sub:ST_CTU + sub + 1],
                                stat[:, ST_YQ + 1:ST_YQ + 2], mw127[:, 1:2], None,
                                AluOpType.mult)
                e.drain()
                e.tensor_scalar(y2b[:, :], y2b[:, :], stat[:, ST_YQ + 3:ST_YQ + 4],
                                None, AluOpType.mult)
                e.tensor_copy(i32y[:, :], y2b[:, :])
                e.tensor_copy(corr1[:, :], i32y[:, :])
                e.tensor_tensor(y2b[:, :], y2b[:, :], corr1[:, :], AluOpType.subtract)
                e.tensor_scalar(corr2[:, :], y2b[:, :], 0.5, None, AluOpType.is_gt)
                e.tensor_scalar(y2b[:, :], y2b[:, :], -0.5, None, AluOpType.is_lt)
                e.tensor_tensor(corr2[:, :], corr2[:, :], y2b[:, :],
                                AluOpType.subtract)
                e.tensor_tensor(y2b[:, :], corr1[:, :], corr2[:, :],
                                AluOpType.add)
                return e.drain()
            mq_ = step("vector", [("se", msq), ("te", tp_mk.get(sub - 1, 0))], nq,
                       "ve")
            ln_mk[sub] = mq_

            psT = psB if sub % 2 == 0 else psC

            def tps(e, psT=psT):
                last = None
                for c in range(8):
                    last = e.matmul(psT[:, c * 128:(c + 1) * 128],
                                    y2b[:, c * 128:(c + 1) * 128], ident[:, :],
                                    is_transpose=True, start=True, stop=True)
                return last
            mtp = step("tensor", [("ve", mq_), ("ve", y2tp_mk.get(sub - 2, dprev))],
                       tps, "te")
            tp_mk[sub] = mtp
            def y2tpf(e, sub=sub, psT=psT):
                e.tensor_copy(y2qT[:, sub * 1024:(sub + 1) * 1024], psT[:, :])
                return e.drain()
            y2tp_mk[sub] = step("vector", [("te", mtp)], y2tpf, "ve")
            if DEBUG and tt == 0 and sub == 3:
                step("gpsimd", [("ve", y2tp_mk[sub])],
                     lambda e: e.dma_start(out=dbg_y2qT, in_=y2qT[:, 0:4096]), "st", 16)

        if "p4ln" in SER:
            barrier()
        gu_te = {}
        gu_se = {}
        gu_ve = {}
        psGU = [psA[:, 0:512], psA[:, 512:1024], psA2[:, 0:512], psA2[:, 512:1024]]
        gidx = 0
        for isl in range(8):
            wgsem = "Lwg0" if isl % 2 == 0 else "Lwg1"

            def ldwgu(e, isl=isl, wgsem=wgsem):
                for c in range(8):
                    dld(e, wgsem, wgb[isl % 2][:, c * 512:(c + 1) * 512],
                        wgq_s[c * 128:(c + 1) * 128, isl * 512:(isl + 1) * 512])
                    dld(e, wgsem, wub[isl % 2][:, c * 512:(c + 1) * 512],
                        wuq_s[c * 128:(c + 1) * 128, isl * 512:(isl + 1) * 512])
                return None
            wld = step("sync", [("te", gu_te.get(isl - 2, dn23_prev))], ldwgu,
                       wgsem, 16 * 16)
            for sub in range(4):
                pg = psGU[gidx % 4]
                wl = [(wgsem, wld), ("ve", y2tp_mk[sub]),
                      ("se", gu_se.get(gidx - 4, 0)), ("ve", gu_ve.get(gidx - 4, 0))]
                if gidx < 4:
                    wl.append(("ve", dprev))

                def gmms(e, isl=isl, sub=sub, pg=pg):
                    last = None
                    for c in range(8):
                        last = e.matmul(
                            pg,
                            y2qT[:, sub * 1024 + c * 128:sub * 1024 + (c + 1) * 128],
                            wgb[isl % 2][:, c * 512:(c + 1) * 512],
                            start=(c == 0), stop=(c == 7))
                    return last
                mg = step("tensor", wl, gmms, "te")
                def siluf(e, isl=isl, sub=sub, pg=pg):
                    e.activation(gh[sub][:, isl * 512:(isl + 1) * 512], pg, AF.Silu,
                                 scale=stat[:, ST_CTG + sub:ST_CTG + sub + 1])
                    return e.drain()
                mge = step("scalar", [("te", mg)], siluf, "se")
                gu_se[gidx] = mge
                pu = psGU[(gidx + 1) % 4]
                wl = [("se", gu_se.get(gidx - 3, 0)), ("ve", gu_ve.get(gidx - 3, 0))]
                if gidx < 3:
                    wl.append(("ve", dprev))

                def umms(e, isl=isl, sub=sub, pu=pu):
                    last = None
                    for c in range(8):
                        last = e.matmul(
                            pu,
                            y2qT[:, sub * 1024 + c * 128:sub * 1024 + (c + 1) * 128],
                            wub[isl % 2][:, c * 512:(c + 1) * 512],
                            start=(c == 0), stop=(c == 7))
                    return last
                mu_ = step("tensor", wl, umms, "te")
                gu_te[isl] = mu_

                def ustt(e, isl=isl, sub=sub, pu=pu):
                    sl = gh[sub][:, isl * 512:(isl + 1) * 512]
                    return e.scalar_tensor_tensor(
                        sl, pu, stat[:, ST_CTU + sub:ST_CTU + sub + 1], sl,
                        AluOpType.mult, AluOpType.mult)
                gu_ve[gidx] = step("vector", [("te", mu_), ("se", mge)], ustt, "ve")
                gidx += 2
        if "p4gu" in SER:
            barrier()
        all_gu_te = CNT["te"]
        all_gu_se = CNT["se"]
        all_gu_ve = CNT["ve"]
        dbg_gh_st = 0
        if DEBUG and tt == 0:
            dbg_gh_st = step("gpsimd", [("ve", all_gu_ve), ("se", all_gu_se)],
                             lambda e: e.dma_start(out=dbg_gh, in_=gh[0][:, :]),
                             "st", 16)

        msc_mk = {}
        for sub in range(4):
            def hscale(e, sub=sub):
                e.reduce_max(stat[:, 51:52], gh[sub][:, :], axis=AXX,
                             apply_absolute_value=True)
                e.drain()
                e.tensor_scalar(stat[:, 52:53], stat[:, 51:52], 1e-5, None,
                                AluOpType.max)
                e.drain()
                e.reciprocal(stat[:, 53:54], stat[:, 52:53])
                e.drain()
                e.tensor_scalar(stat[:, ST_HQ + sub:ST_HQ + sub + 1],
                                stat[:, 53:54], 127.0, None, AluOpType.mult)
                e.tensor_scalar(stat[:, ST_C2 + sub:ST_C2 + sub + 1],
                                stat[:, 52:53], mw127[:, 2:3],
                                None, AluOpType.mult)
                return e.drain()
            msc_mk[sub] = step("vector", [("ve", all_gu_ve)], hscale, "ve")

        fbufs = [arf[:, 25600:27648], arf[:, 27648:29696]]
        hw_mk = {}
        cast_mk = {}
        tp_mk2 = {}
        ev_mk = {}
        HQI = [(sub, half) for sub in range(4) for half in range(2)]

        def emit_cast(idx):
            sub, half = HQI[idx]

            def castf(e, sub=sub, half=half, idx=idx):
                sl = gh[sub][:, half * 2048:(half + 1) * 2048]
                e.activation(i32t[:, :], sl, AF.Copy,
                             scale=stat[:, ST_HQ + sub:ST_HQ + sub + 1])
                e.drain()
                e.activation(fbufs[idx % 2][:, :], i32t[:, :], AF.Copy)
                return e.drain()
            cwl = [("ve", msc_mk[sub]), ("ve", hw_mk.get(idx - 2, 0))]
            if idx == 0:
                cwl.append(("st", dbg_gh_st))
            cast_mk[idx] = step("scalar", cwl, castf, "se")

        emit_cast(0)
        hcnt = 0
        for idx, (sub, half) in enumerate(HQI):
            if idx + 1 < 8:
                emit_cast(idx + 1)

            def chainf(e, sub=sub, half=half, idx=idx):
                sl = gh[sub][:, half * 2048:(half + 1) * 2048]
                fb = fbufs[idx % 2][:, :]
                e.scalar_tensor_tensor(sl, sl,
                                       stat[:, ST_HQ + sub:ST_HQ + sub + 1],
                                       fb, AluOpType.mult, AluOpType.subtract)
                e.scalar_tensor_tensor(fb, sl, 0.5, fb,
                                       AluOpType.is_gt, AluOpType.add)
                e.tensor_scalar(sl, sl, -0.5, None, AluOpType.is_lt)
                e.tensor_tensor(sl, fb, sl, AluOpType.subtract)
                return e.drain()
            hw_mk[idx] = step("vector", [("se", cast_mk[idx])], chainf, "ve")
            for cc in range(4):
                    psT = psB[:, (hcnt % 2) * 512:(hcnt % 2) * 512 + 512]
                    isl = half * 4 + cc

                    def hq2(e, sub=sub, isl=isl, psT=psT):
                        last = None
                        for c in range(4):
                            last = e.matmul(
                                psT[:, c * 128:(c + 1) * 128],
                                gh[sub][:, isl * 512 + c * 128:isl * 512 + (c + 1) * 128],
                                ident[:, :],
                                is_transpose=True, start=True, stop=True)
                        return last
                    tp_mk2[hcnt] = step("tensor",
                                        [("ve", hw_mk[idx]),
                                         ("se", ev_mk.get(hcnt - 2, 0)),
                                         ("ve", dprev if hcnt == 0 else 0)],
                                        hq2, "te")

                    def hq3(e, sub=sub, isl=isl, psT=psT):
                        off = sub * 4096 + isl * 512
                        e.activation(hqT[:, off:off + 512], psT, AF.Copy)
                        return e.drain()
                    ev_mk[hcnt] = step("scalar", [("te", tp_mk2[hcnt])], hq3,
                                       "se")
                    hcnt += 1

        all_tp_ve = CNT["ve"]
        all_tp_se = CNT["se"]
        psDN = [[psA[:, 0:512], psA[:, 512:1024]],
                [psA2[:, 0:512], psA2[:, 512:1024]],
                [psC[:, 0:512], psC[:, 512:1024]],
                [psB[:, 0:512], psB[:, 512:1024]]]
        dn23_te = {}
        dwld2 = {}
        for icg in range(8):
            wdsem = "Lwd0" if icg % 2 == 0 else "Lwd1"

            def lddw2(e, icg=icg, wdsem=wdsem):
                for cc in range(4):
                    dld(e, wdsem, wdb[icg % 2][:, cc * 1024:(cc + 1) * 1024],
                        wdq_s[icg * 512 + cc * 128:icg * 512 + (cc + 1) * 128, :])
                return None
            wl = [("te", dn23_te.get(icg - 2, dn23_prev if icg < 2 else 0))]
            dwld2[icg] = step("sync", wl, lddw2, wdsem, 16 * 4)
            wl = [(wdsem, dwld2[icg]), ("ve", all_tp_ve), ("se", all_tp_se)]

            def dmms23(e, icg=icg):
                last = None
                for sub in range(4):
                    for ic4 in range(4):
                        for osl in range(2):
                            last = e.matmul(
                                psDN[sub][osl],
                                hqT[:, sub * 4096 + (icg * 4 + ic4) * 128:
                                    sub * 4096 + (icg * 4 + ic4) * 128 + 128],
                                wdb[icg % 2][:, ic4 * 1024 + osl * 512:
                                             ic4 * 1024 + (osl + 1) * 512],
                                start=(icg == 0 and ic4 == 0),
                                stop=(icg == 7 and ic4 == 3))
                return last
            dn23_te[icg] = step("tensor", wl, dmms23, "te")

        dn_ev = {}
        dn_st = {}
        for sub in range(4):
            xrsem = "Lxr0" if sub % 2 == 0 else "Lxr1"
            mxr = step("scalar", [("ve", dn_ev.get(sub - 2, dprev))],
                       lambda e, tt=tt, sub=sub: e.dma_start(
                           out=xmr[sub % 2][:, :],
                           in_=xmid_s[(tt * 4 + sub) * 128:(tt * 4 + sub + 1) * 128, :]),
                       xrsem, 16)

            def dnev(e, sub=sub):
                for osl in range(2):
                    e.scalar_tensor_tensor(
                        xout[:, osl * 512:(osl + 1) * 512],
                        psDN[sub][osl],
                        stat[:, ST_C2 + sub:ST_C2 + sub + 1],
                        xmr[sub % 2][:, osl * 512:(osl + 1) * 512],
                        AluOpType.mult, AluOpType.add)
                return e.drain()
            wl = [("te", dn23_te[7]), (xrsem, mxr), ("Sy", dn_st.get(sub - 1, sprev))]
            dn_ev[sub] = step("vector", wl, dnev, "ve")
            dn_st[sub] = step("gpsimd", [("ve", dn_ev[sub])],
                              lambda e, tt=tt, sub=sub: e.dma_start(
                                  out=y_own[(tt * 4 + sub) * 128:(tt * 4 + sub + 1) * 128, :],
                                  in_=xout[:, :]), "Sy", 16)
        if "p4dn" in SER:
            barrier()
        dn_ev_prev = dn_ev[3]
        dn_st_prev = dn_st[3]
        dn23_prev = dn23_te[7]
        ln_prev = ln_mk
        flush(block)

    step("gpsimd", [(n, CNT[n]) for n in SSEMS], lambda e: e.nop(), "ge")
    flush(block)
    blk.__exit__(None, None, None)
    return nc


_NC = None


def kernel(**inputs):
    global _NC
    x = np.asarray(inputs["L0_attn_x"], dtype=np.float32)
    wq = np.ascontiguousarray(np.asarray(inputs["L0_attn_W_Q"], np.float32))
    wk = np.ascontiguousarray(np.asarray(inputs["L0_attn_W_K"], np.float32))
    wv = np.ascontiguousarray(np.asarray(inputs["L0_attn_W_V"], np.float32))
    wo = np.ascontiguousarray(np.asarray(inputs["L0_attn_W_o"], np.float32))
    wgT = np.ascontiguousarray(np.asarray(inputs["L0_ffn_w_gate"], np.float32).T)
    wuT = np.ascontiguousarray(np.asarray(inputs["L0_ffn_w_up"], np.float32).T)
    wdT = np.ascontiguousarray(np.asarray(inputs["L0_ffn_w_down"], np.float32).T)

    blocks = (B0, B1)
    dl = {}
    for par in range(2):
        d = np.zeros((128, NM), np.float32)
        for i, (q, kc, r) in enumerate(MASKS):
            g = blocks[par][4 * q + r]
            d[:, i] = 128.0 * (kc - g)
        dl[par] = d

    in_maps = []
    for c in range(8):
        seq, par = c // 2, c % 2
        rows = np.concatenate([np.arange(b * 128, (b + 1) * 128)
                               for b in blocks[par]])
        in_maps.append({
            "x_full": np.ascontiguousarray(x[seq]),
            "x_own": np.ascontiguousarray(x[seq][rows]),
            "deltas": dl[par],
            "wq_in": wq, "wk_in": wk, "wv_in": wv, "wo_in": wo,
            "wgT_in": wgT, "wuT_in": wuT, "wdT_in": wdT,
        })

    if _NC is None:
        _NC = build_nc()
    res = run_bass_kernel_spmd(_NC, in_maps, core_ids=list(range(8)))

    y = np.empty((4, S, D), np.float32)
    for c in range(8):
        seq, par = c // 2, c % 2
        out = np.asarray(res.results[c]["y_own"])
        for r, b in enumerate(blocks[par]):
            y[seq, b * 128:(b + 1) * 128, :] = out[r * 128:(r + 1) * 128, :]
    return y


if __name__ == "__main__":
    import time
    t0 = time.time()
    nc = build_nc()
    print(f"build ok in {time.time() - t0:.1f}s")


# version 21
# speedup vs baseline: 1.0977x; 1.0113x over previous
import numpy as np
import concourse.bass as bass
import concourse.mybir as mybir
from concourse.bass_utils import run_bass_kernel_spmd
from concourse.alu_op_type import AluOpType

F32 = mybir.dt.float32
BF16 = mybir.dt.bfloat16
I32 = mybir.dt.int32
AF = mybir.ActivationFunctionType
AXX = mybir.AxisListType.X

D = 1024
S = 4096
OT = 2048
NQ = 4
ENGINES = ("sync", "gpsimd", "tensor", "vector", "scalar")
DEBUG = False
DEBUG2 = False
SER = set()


def _blocks(par):
    bs = []
    for j in range(16):
        if j % 2 == par:
            bs.append(j)
            bs.append(31 - j)
    return sorted(bs)


B0 = _blocks(0)
B1 = _blocks(1)
MAXG = [max(B0[r], B1[r]) for r in range(16)]
MING = [min(B0[r], B1[r]) for r in range(16)]
KCQ = [MAXG[4 * q + 3] + 1 for q in range(NQ)]
KGQ = [kc // 2 for kc in KCQ]


def _suffix(q, kc):
    for r in range(4):
        if MAXG[4 * q + r] >= kc:
            return r
    raise AssertionError


MASKS = []
MIDX = {}
for _q in range(NQ):
    for _r in range(4):
        for _kc in range(MING[4 * _q + _r], MAXG[4 * _q + _r] + 1):
            MIDX[(_q, _kc, _r)] = len(MASKS)
            MASKS.append((_q, _kc, _r))
NM = len(MASKS)
assert NM == 32, NM

ST_LN = 4
ST_WQ = 30
ST_CTG = 35
ST_CTU = 39
ST_C2 = 43
ST_HQ = 47
ST_YQ = 51


def build_nc():
    nc = bass.Bass("TRN2", target_bir_lowering=False, detect_race_conditions=False)

    x_full = nc.dram_tensor("x_full", [S, D], F32, kind="ExternalInput").ap()
    x_own = nc.dram_tensor("x_own", [OT, D], F32, kind="ExternalInput").ap()
    deltas = nc.dram_tensor("deltas", [128, NM], F32, kind="ExternalInput").ap()
    wq_in = nc.dram_tensor("wq_in", [D, D], F32, kind="ExternalInput").ap()
    wk_in = nc.dram_tensor("wk_in", [D, D], F32, kind="ExternalInput").ap()
    wv_in = nc.dram_tensor("wv_in", [D, D], F32, kind="ExternalInput").ap()
    wo_in = nc.dram_tensor("wo_in", [D, D], F32, kind="ExternalInput").ap()
    wgT_in = nc.dram_tensor("wgT_in", [D, 4096], F32, kind="ExternalInput").ap()
    wuT_in = nc.dram_tensor("wuT_in", [D, 4096], F32, kind="ExternalInput").ap()
    wdT_in = nc.dram_tensor("wdT_in", [4096, D], F32, kind="ExternalInput").ap()
    y_own = nc.dram_tensor("y_own", [OT, D], F32, kind="ExternalOutput").ap()

    y1T_s = nc.dram_tensor("y1T_s", [D, S], F32, kind="Internal").ap()
    y1To_s = nc.dram_tensor("y1To_s", [D, OT], F32, kind="Internal").ap()
    KT_s = nc.dram_tensor("KT_s", [D, S], F32, kind="Internal").ap()
    V_s = nc.dram_tensor("V_s", [32, 128, 1040], F32, kind="Internal").ap()
    QT_s = nc.dram_tensor("QT_s", [D, OT], F32, kind="Internal").ap()
    xmid_s = nc.dram_tensor("xmid_s", [OT, D], F32, kind="Internal").ap()
    wgq_s = nc.dram_tensor("wgq_s", [D, 4096], BF16, kind="Internal").ap()
    if DEBUG2:
        dbg_xmid2 = nc.dram_tensor("dbg_xmid2", [OT, D], F32, kind="ExternalOutput").ap()
        dbg_wgq = nc.dram_tensor("dbg_wgq", [D, 4096], BF16, kind="ExternalOutput").ap()
        dbg_wuq = nc.dram_tensor("dbg_wuq", [D, 4096], BF16, kind="ExternalOutput").ap()
        dbg_wdq = nc.dram_tensor("dbg_wdq", [4096, D], BF16, kind="ExternalOutput").ap()
    if DEBUG:
        dbg_y2qT = nc.dram_tensor("dbg_y2qT", [128, 4096], BF16, kind="ExternalOutput").ap()
        dbg_gh = nc.dram_tensor("dbg_gh", [128, 4096], F32, kind="ExternalOutput").ap()
        dbg_hqT = nc.dram_tensor("dbg_hqT", [128, 4096], BF16, kind="ExternalOutput").ap()
        dbg_stat = nc.dram_tensor("dbg_stat", [128, 64], F32, kind="ExternalOutput").ap()
        dbg_xmid = nc.dram_tensor("dbg_xmid", [OT, D], F32, kind="ExternalOutput").ap()
        dbg_y1T = nc.dram_tensor("dbg_y1T", [D, S], F32, kind="ExternalOutput").ap()
        dbg_KT = nc.dram_tensor("dbg_KT", [D, 512], F32, kind="ExternalOutput").ap()
        dbg_QT = nc.dram_tensor("dbg_QT", [D, 512], F32, kind="ExternalOutput").ap()
        dbg_P = nc.dram_tensor("dbg_P", [128, 1024], F32, kind="ExternalOutput").ap()
    wuq_s = nc.dram_tensor("wuq_s", [D, 4096], BF16, kind="Internal").ap()
    wdq_s = nc.dram_tensor("wdq_s", [4096, D], BF16, kind="Internal").ap()

    ARF = 30000
    ARB = 31000
    arf = nc.alloc_sbuf_tensor("arf", [128, ARF], F32)
    arb = nc.alloc_sbuf_tensor("arb", [128, ARB], BF16)
    ident = nc.alloc_sbuf_tensor("ident", [128, 128], F32)
    iof = nc.alloc_sbuf_tensor("iof", [128, 128], F32)
    iop = nc.alloc_sbuf_tensor("iop", [128, 1], F32)
    shiftm = nc.alloc_sbuf_tensor("shiftm", [64, 128], F32)
    onesrow = nc.alloc_sbuf_tensor("onesrow", [1, 128], F32)
    onescol = nc.alloc_sbuf_tensor("onescol", [128, 1], F32)
    bdg = nc.alloc_sbuf_tensor("bdg", [128, 2], F32)
    negmb = nc.alloc_sbuf_tensor("negmb", [128, 1], F32)
    mw127 = nc.alloc_sbuf_tensor("mw127", [128, 3], F32)
    dlt = nc.alloc_sbuf_tensor("dlt", [128, NM], F32)
    i32t = nc.alloc_sbuf_tensor("i32t", [128, 2048], I32)
    i32y = nc.alloc_sbuf_tensor("i32y", [128, 1024], I32)
    stat = nc.alloc_sbuf_tensor("stat", [128, 56], F32)
    ss = nc.alloc_sbuf_tensor("ss", [2, 104], F32)
    sm = nc.alloc_sbuf_tensor("sm", [1, 24], F32)
    identb = nc.alloc_sbuf_tensor("identb", [128, 128], BF16)
    epsb = nc.alloc_sbuf_tensor("epsb", [128, 1], F32)

    psA = nc.alloc_psum_tensor("psA", [128, 1024], F32)
    psA2 = nc.alloc_psum_tensor("psA2", [128, 1024], F32)
    psB = nc.alloc_psum_tensor("psB", [128, 1024], F32)
    psC = nc.alloc_psum_tensor("psC", [128, 1024], F32)

    LSEMS = ("Lxb0", "Lxb1", "Lqw0", "Lqw1", "Lyb0", "Lyb1", "Lwb", "Lkv0",
             "Lkv1", "Lqt", "Lwo", "Lxo0", "Lxo1", "Lxl0", "Lxl1", "Lwg0",
             "Lwg1", "Lwd0", "Lwd1", "Ldlt", "Lxr0", "Lxr1", "Lxb2", "Lxb3")
    SSEMS = ("Syst0", "Syst1", "Syst2", "Syst3", "Svst0", "Svst1", "Sqst0",
             "Sqst1", "Swqt0", "Swqt1", "Sxm0", "Sxm1", "Sxout", "Sy", "st")
    SEM = {}
    for n in ("ld", "te", "ve", "se", "ge", "sy") + LSEMS + SSEMS:
        SEM[n] = nc.alloc_semaphore(name=n)
    CNT = dict.fromkeys(SEM, 0)

    queues = {e: [] for e in ENGINES}

    def step(eng, waits, emitfn, incsem=None, amt=1):
        mark = None
        if incsem is not None:
            CNT[incsem] += amt
            mark = CNT[incsem]
        wl = [(s, v) for (s, v) in waits if v and v > 0]

        def fn(e, wl=wl, emitfn=emitfn, incsem=incsem, amt=amt):
            for s, v in wl:
                e.wait_ge(SEM[s], v)
            ins = emitfn(e)
            if incsem is not None and ins is not None:
                ins.then_inc(SEM[incsem], amt)
        queues[eng].append(fn)
        return mark

    def dld(e, sem, out, in_):
        e.dma_start(out=out, in_=in_).then_inc(SEM[sem], 16)

    def barrier():
        lwl = [(n, CNT[n]) for n in LSEMS + ("ld",)] + [("sy", CNT["sy"])]
        swl = [(n, CNT[n]) for n in SSEMS] + [("ge", CNT["ge"])]
        step("sync", lwl, lambda e: e.nop(), "sy")
        step("gpsimd", swl, lambda e: e.nop(), "ge")
        step("tensor", [("te", CNT["te"])], lambda e: e.nop(), "te")
        step("vector", [("ve", CNT["ve"])], lambda e: e.nop(), "ve")
        step("scalar", [("se", CNT["se"])], lambda e: e.nop(), "se")
        tg = {s: CNT[s] for s in ("sy", "ge", "te", "ve", "se")}
        own = {"sync": "sy", "gpsimd": "ge", "tensor": "te", "vector": "ve",
               "scalar": "se"}
        for eng, osem in own.items():
            wl = [(s, tg[s]) for s in tg if s != osem]
            step(eng, wl, lambda e: None, None)

    def flush(block):
        for eng in ENGINES:
            fns = queues[eng]
            queues[eng] = []
            if not fns:
                continue
            sec = getattr(block, eng)

            def body(e, fns=fns):
                for f in fns:
                    f(e)
            sec(body)

    blk = nc.Block()
    block = blk.__enter__()

    m_ld = step("sync", [], lambda e: e.dma_start(out=dlt[:, :], in_=deltas[:, :]),
                "Ldlt", 16)
    diffm = arf[:, 0:128]

    def p0_gp(e):
        e.iota(iof[:, :], pattern=[[1, 128]], base=0, channel_multiplier=0,
               allow_small_or_imprecise_dtypes=True)
        e.memset(onesrow[:, :], 1.0)
        e.memset(epsb[:, :], 1e-5)
        e.memset(onescol[:, :], 1.0)
        return e.iota(iop[:, :], pattern=[[0, 1]], base=0, channel_multiplier=1,
                      allow_small_or_imprecise_dtypes=True)
    m_ge = step("gpsimd", [], p0_gp, "ge")
    barrier()

    maskb = arb[:, 0:NM * 128]

    def p0_ve(e):
        e.tensor_scalar(ident[:, :], iof[:, :], iop[:, :], None, AluOpType.is_equal)
        e.drain()
        e.tensor_copy(identb[:, :], ident[:, :])
        e.tensor_scalar(stat[:, 0:1], iop[:, :], 64.0, None, AluOpType.add)
        e.drain()
        e.tensor_scalar(shiftm[:, :], iof[0:64, :], stat[0:64, 0:1], None,
                        AluOpType.is_equal)
        e.tensor_scalar(diffm, iof[:, :], iop[:, :], None, AluOpType.subtract)
        e.tensor_scalar(bdg[:, 0:1], iop[:, :], 64.0, None, AluOpType.is_lt)
        e.tensor_scalar(bdg[:, 1:2], iop[:, :], 64.0, None, AluOpType.is_ge)
        e.memset(psA[:, :], 0.0)
        e.memset(psA2[:, :], 0.0)
        for i in range(NM):
            e.tensor_scalar(maskb[:, i * 128:(i + 1) * 128], diffm,
                            dlt[:, i:i + 1], None, AluOpType.is_ge)
        return e.drain()
    step("vector", [("Ldlt", m_ld)], p0_ve, "ve")
    barrier()

    xb = [arf[:, i * 1024:(i + 1) * 1024] for i in range(4)]
    sqb = arf[:, 4096:5120]
    ybp = [arf[:, 5120 + i * 1024:5120 + (i + 1) * 1024] for i in range(4)]
    yst = [arf[:, 9216 + i * 1024:9216 + (i + 1) * 1024] for i in range(4)]
    wt = [arf[:, 7168:11264], arf[:, 11264:15360]]
    c1b = arf[:, 15360:19456]
    c2b = arf[:, 19456:23552]
    wqt = arb[:, 4096:8192]

    def emit_ln(src_ap, t, xbuf, ybuf, sb, prevp):
        mk = {}
        xbsem = "Lxb%d" % (t % 4)
        mk["ld"] = step("sync", [("ve", prevp.get("y1", 0))],
                        lambda e: e.dma_start(out=xbuf[:, :],
                                              in_=src_ap[t * 128:(t + 1) * 128, :]),
                        xbsem, 16)

        def acts(e):
            e.activation(sqb[:, :], xbuf[:, :], AF.Square,
                         accum_out=stat[:, sb:sb + 1])
            e.activation(sqb[:, :], xbuf[:, :], AF.Copy,
                         accum_out=stat[:, sb + 1:sb + 2])
            return e.drain()
        mk["acc"] = step("scalar", [(xbsem, mk["ld"]), ("ve", prevp.get("var", 0))],
                         acts, "se")

        def stats(e):
            e.tensor_scalar(stat[:, sb + 2:sb + 3], stat[:, sb + 1:sb + 2],
                            1.0 / 1024, None, AluOpType.mult)
            e.drain()
            e.tensor_tensor(stat[:, sb + 4:sb + 5], stat[:, sb + 2:sb + 3],
                            stat[:, sb + 2:sb + 3], AluOpType.mult)
            e.drain()
            e.scalar_tensor_tensor(stat[:, sb + 6:sb + 7],
                                       stat[:, sb:sb + 1], 1.0 / 1024,
                                       stat[:, sb + 4:sb + 5],
                                       AluOpType.mult, AluOpType.subtract)
            return e.drain()
        mk["var"] = step("vector", [("se", mk["acc"])], stats, "ve")
        def sqrtf(e):
            e.activation(stat[:, sb + 7:sb + 8], stat[:, sb + 6:sb + 7], AF.Sqrt,
                         bias=epsb[:, :])
            return e.drain()
        mk["sqrt"] = step("scalar", [("ve", mk["var"]), ("ve", prevp.get("y1", 0))],
                          sqrtf, "se")

        def norm(e):
            e.reciprocal(stat[:, sb + 8:sb + 9], stat[:, sb + 7:sb + 8])
            e.drain()
            e.tensor_scalar(ybuf[:, :], xbuf[:, :], stat[:, sb + 2:sb + 3],
                            stat[:, sb + 8:sb + 9],
                            AluOpType.subtract, AluOpType.mult)
            return e.drain()
        mk["y1"] = step("vector", [("se", mk["sqrt"]), ("te", prevp.get("tp", 0))],
                        norm, "ve")

        ps = (psB, psC, psA, psA2)[t % 4]

        def tps(e):
            last = None
            for c in range(8):
                last = e.matmul(ps[:, c * 128:(c + 1) * 128],
                                ybuf[:, c * 128:(c + 1) * 128], ident[:, :],
                                is_transpose=True, start=True, stop=True)
            return last
        mk["tp"] = step("tensor", [("ve", mk["y1"]), ("ve", prevp.get("ev", 0))],
                        tps, "te")
        ystb = yst[t % 4]
        def evf(e):
            e.tensor_copy(ystb[:, :], ps[:, :])
            return e.drain()
        ystsem = "Syst%d" % (t % 4)
        mk["ev"] = step("vector", [("te", mk["tp"]), (ystsem, prevp.get("st", 0))],
                        evf, "ve")
        return mk, ystb

    def ln_stream(src_ap, ntiles, dsts, prev=None):
        if prev is None:
            prev = [{}, {}, {}, {}]
        for t in range(ntiles):
            sb = ST_LN + (t % 4) * 13
            mk, ystb = emit_ln(src_ap, t, xb[t % 4], ybp[t % 4], sb, prev[t % 4])

            stsem = "Syst%d" % (t % 4)

            def store(e, t=t, ystb=ystb, stsem=stsem):
                for c in range(8):
                    dld(e, stsem, dsts[c * 128:(c + 1) * 128, t * 128:(t + 1) * 128],
                        ystb[:, c * 128:(c + 1) * 128])
                return None
            mk["st"] = step("gpsimd", [("ve", mk["ev"])], store, stsem, 16 * 8)
            prev[t % 4] = mk
        return prev

    _lnp = ln_stream(x_full, 32, y1T_s)
    ln_stream(x_own, 16, y1To_s, _lnp)


    barrier()
    flush(block)

    wbuf = arf[:, 0:8192]
    ybuf = [arf[:, 8192:12288], arf[:, 12288:16384]]
    qst = [arf[:, 16384:16896], arf[:, 16896:17408]]
    sqt = [arf[:, 17408:17920], arf[:, 17920:18432]]
    vst = [arf[:, 18432:19472], arf[:, 19472:20512]]
    wtq = [arf[:, 20512:22560], arf[:, 22560:24608]]
    c1b = arf[:, 24608:26656]
    c2b = arf[:, 26656:28704]
    wqt2 = [arb[:, 4096:6144], arb[:, 6144:8192]]

    nidx = [0]
    pb_prev = {}
    pc_prev = {}
    st_prev = {}
    bd_prev = {}
    vb_prev = {}
    vst_st = {}

    def quant_units():
        carry_qz = 0
        carry_st = {}
        for (src_ap, ntiles, tile_w, dsts, mwcol) in [
                (wgT_in, 8, 4096, wgq_s, 0),
                (wuT_in, 8, 4096, wuq_s, 1),
                (wdT_in, 32, 1024, wdq_s, 2)]:
            cw = min(tile_w, 2048)
            nch = tile_w // cw
            step("vector", [], lambda e: e.memset(stat[:, ST_WQ:ST_WQ + 1], 0.0),
                 "ve")
            prev_red = [carry_qz, carry_qz]
            ui = 0
            for t in range(ntiles):
                for c in range(nch):
                    b = ui % 2
                    qsem = "Lqw0" if b == 0 else "Lqw1"
                    mld = step("scalar", [("ve", prev_red[b])],
                               lambda e, t=t, c=c, b=b, cw=cw, src_ap=src_ap: e.dma_start(
                                   out=wtq[b][:, 0:cw],
                                   in_=src_ap[t * 128:(t + 1) * 128,
                                              c * cw:(c + 1) * cw]), qsem, 16)

                    def red(e, b=b, cw=cw):
                        e.reduce_sum(stat[:, ST_WQ + 1:ST_WQ + 2], wtq[b][:, 0:cw],
                                     axis=AXX, apply_absolute_value=True)
                        e.drain()
                        e.tensor_tensor(stat[:, ST_WQ:ST_WQ + 1],
                                        stat[:, ST_WQ:ST_WQ + 1],
                                        stat[:, ST_WQ + 1:ST_WQ + 2],
                                        AluOpType.add)
                        return e.drain()
                    prev_red[b] = step("vector", [(qsem, mld)], red, "ve")
                    ui += 1
                    yield None
            mA = step("tensor", [("ve", max(prev_red)), ("ve", pc_prev.get(0, 0))],
                      lambda e: e.matmul(psC[0:1, 0:1], stat[:, ST_WQ:ST_WQ + 1],
                                         onescol[:, :], start=True, stop=True),
                      "te")

            def mk_m(e):
                e.tensor_scalar(sm[:, 8:9], psC[0:1, 0:1], 1.0 / 4194304, None,
                                AluOpType.mult)
                e.drain()
                e.tensor_scalar(sm[:, 9:10], sm[0:1, 8:9], 1e-5, None,
                                AluOpType.max)
                return e.drain()
            mB = step("vector", [("te", mA)], mk_m, "ve")
            pc_prev[0] = mB
            mC = step("tensor", [("ve", mB), ("ve", pc_prev.get(1, 0))],
                      lambda e: e.matmul(psC[:, 512:513], onesrow[0:1, :],
                                         sm[0:1, 9:10], start=True, stop=True),
                      "te")

            def thr(e, mwcol=mwcol):
                e.tensor_copy(stat[:, ST_WQ + 2:ST_WQ + 3], psC[:, 512:513])
                e.drain()
                e.tensor_scalar(stat[:, ST_WQ + 3:ST_WQ + 4],
                                stat[:, ST_WQ + 2:ST_WQ + 3], 0.5, None,
                                AluOpType.mult)
                e.drain()
                e.tensor_scalar(stat[:, ST_WQ + 4:ST_WQ + 5],
                                stat[:, ST_WQ + 3:ST_WQ + 4], -1.0, None,
                                AluOpType.mult)
                return e.tensor_scalar(mw127[:, mwcol:mwcol + 1],
                                       stat[:, ST_WQ + 2:ST_WQ + 3],
                                       1.0 / 127, None, AluOpType.mult)
            mD = step("vector", [("te", mC)], thr, "ve")
            pc_prev[1] = mD
            yield None
            prev_q = [mD, mD]
            prev_st = {}
            ui = 0
            for t in range(ntiles):
                for c in range(nch):
                    b = ui % 2
                    qsem = "Lqw0" if b == 0 else "Lqw1"
                    mld = step("scalar", [("ve", prev_q[b])],
                               lambda e, t=t, c=c, b=b, cw=cw, src_ap=src_ap: e.dma_start(
                                   out=wtq[b][:, 0:cw],
                                   in_=src_ap[t * 128:(t + 1) * 128,
                                              c * cw:(c + 1) * cw]), qsem, 16)

                    def qz(e, b=b, cw=cw):
                        e.tensor_scalar(c1b[:, 0:cw], wtq[b][:, 0:cw],
                                        stat[:, ST_WQ + 3:ST_WQ + 4], None,
                                        AluOpType.is_gt)
                        e.tensor_scalar(c2b[:, 0:cw], wtq[b][:, 0:cw],
                                        stat[:, ST_WQ + 4:ST_WQ + 5], None,
                                        AluOpType.is_lt)
                        e.drain()
                        e.tensor_tensor(wqt2[b][:, 0:cw], c1b[:, 0:cw],
                                        c2b[:, 0:cw], AluOpType.subtract)
                        return e.drain()
                    wsem = "Swqt0" if b == 0 else "Swqt1"
                    wl = [(qsem, mld), (wsem, prev_st.get(b, carry_st.get(b, 0)))]
                    mq_ = step("vector", wl, qz, "ve")
                    prev_q[b] = mq_
                    stp = step("gpsimd", [("ve", mq_)],
                               lambda e, t=t, c=c, b=b, cw=cw, dsts=dsts: e.dma_start(
                                   out=dsts[t * 128:(t + 1) * 128,
                                            c * cw:(c + 1) * cw],
                                   in_=wqt2[b][:, 0:cw]), wsem, 16)
                    prev_st[b] = stp
                    carry_st[b] = stp
                    ui += 1
                    yield None
            carry_qz = max(prev_q)

    QG = quant_units()
    qg_done = [False]

    def pump(n):
        for _ in range(n):
            if qg_done[0]:
                return
            try:
                next(QG)
            except StopIteration:
                qg_done[0] = True
                return

    def loadW(dstb, src, wait_te):
        def f(e):
            for c in range(8):
                dld(e, "Lwb", dstb[:, c * 1024:(c + 1) * 1024],
                    src[c * 128:(c + 1) * 128, :])
            return None
        return step("sync", [("te", wait_te)], f, "Lwb", 16 * 8)

    def vst_ones(e):
        for t in range(2):
            vv = vst[t].rearrange("p (h d) -> p h d", d=65)
            e.memset(vv[:, :, 64:65], 1.0)
        return e.drain()
    m_ones = step("vector", [], vst_ones, "ve")

    def proj_block(mk_deps, wbufx, ybufg, db, store_fn):
        par = db % 2
        pslice = psB[:, par * 512:(par + 1) * 512]
        wl = list(mk_deps) + [("se", pb_prev.get(par, 0))]

        def mms(e):
            last = None
            for c in range(8):
                last = e.matmul(pslice,
                                wbufx[:, c * 1024 + (db % 8) * 128:c * 1024 + (db % 8) * 128 + 128],
                                ybufg[:, c * 512:(c + 1) * 512],
                                start=(c == 0), stop=(c == 7))
            return last
        mk_mm = step("tensor", wl, mms, "te")

        def evac(e):
            e.copy(qst[par][:, :], pslice)
            e.drain()
            return e.activation(sqt[par][:, :], qst[par][:, :], AF.Square)
        qsem = "Sqst0" if par == 0 else "Sqst1"
        mk_ev = step("scalar", [("te", mk_mm), (qsem, st_prev.get(par, 0)),
                                ("te", bd_prev.get(par, 0))], evac, "se")
        pb_prev[par] = mk_ev
        st_prev[par] = step("gpsimd", [("se", mk_ev)],
                            lambda e: store_fn(e, qst[par]), qsem, 16)
        cpar = nidx[0] % 2
        cslice = psC[0:2, cpar * 512:cpar * 512 + 512]
        mk_bd = step("tensor", [("se", mk_ev), ("ve", pc_prev.get(cpar, 0))],
                     lambda e: e.matmul(cslice, bdg[:, :], sqt[par][:, :],
                                        start=True, stop=True), "te")
        bd_prev[par] = mk_bd
        col = nidx[0]
        pc_prev[cpar] = step("vector", [("te", mk_bd)],
                             lambda e: e.reduce_max(ss[0:2, col:col + 1], cslice,
                                                    axis=AXX), "ve")
        nidx[0] += 1
        return mk_mm

    mW = loadW(wbuf, wk_in, 0)
    grp_last = {}
    for g in range(8):
        ybsem = "Lyb0" if g % 2 == 0 else "Lyb1"

        def ldy(e, g=g, ybsem=ybsem):
            for c in range(8):
                dld(e, ybsem, ybuf[g % 2][:, c * 512:(c + 1) * 512],
                    y1T_s[c * 128:(c + 1) * 128, g * 512:(g + 1) * 512])
            return None
        mldy = step("sync", [("te", grp_last.get(g - 2, 0))], ldy, ybsem, 16 * 8)
        for db in range(8):
            proj_block([(ybsem, mldy), ("Lwb", mW)], wbuf, ybuf[g % 2], db,
                       lambda e, buf, g=g, db=db: e.dma_start(
                           out=KT_s[db * 128:(db + 1) * 128, g * 512:(g + 1) * 512],
                           in_=buf[:, :]))
        grp_last[g] = CNT["te"]
        pump(7)

    mW = loadW(wbuf, wv_in, CNT["te"])
    grp_last = {}
    for g in range(8):
        ybsem = "Lyb0" if g % 2 == 0 else "Lyb1"

        def ldy(e, g=g, ybsem=ybsem):
            for c in range(8):
                dld(e, ybsem, ybuf[g % 2][:, c * 512:(c + 1) * 512],
                    y1T_s[c * 128:(c + 1) * 128, g * 512:(g + 1) * 512])
            return None
        mldy = step("sync", [("te", grp_last.get(g - 2, 0))], ldy, ybsem, 16 * 8)
        for tc in range(4):
            for sl in range(2):
                idx = tc * 2 + sl
                pv = (psA if idx % 2 == 0 else psA2)[:, 0:512]
                wl = [(ybsem, mldy), ("Lwb", mW), ("ve", vb_prev.get(idx % 2, 0))]

                def vmms(e, g=g, tc=tc, sl=sl, pv=pv):
                    last = None
                    for c in range(8):
                        last = e.matmul(
                            pv,
                            ybuf[g % 2][:, c * 512 + tc * 128:c * 512 + tc * 128 + 128],
                            wbuf[:, c * 1024 + sl * 512:c * 1024 + (sl + 1) * 512],
                            start=(c == 0), stop=(c == 7))
                    return last
                mk_vm = step("tensor", wl, vmms, "te")
                vsem = "Svst0" if tc % 2 == 0 else "Svst1"
                wl2 = [("te", mk_vm), ("ve", m_ones),
                       (vsem, vst_st.get(tc % 2, 0) if sl == 0 else 0)]

                def vev(e, tc=tc, sl=sl, pv=pv):
                    vv = vst[tc % 2].rearrange("p (h d) -> p h d", d=65)
                    pvv = pv.rearrange("p (h d) -> p h d", d=64)
                    e.tensor_copy(vv[:, sl * 8:(sl + 1) * 8, 0:64], pvv)
                    return e.drain()
                mk_ve = step("vector", wl2, vev, "ve")
                vb_prev[idx % 2] = mk_ve
                if sl == 1:
                    vst_st[tc % 2] = step(
                        "gpsimd", [("ve", mk_ve)],
                        lambda e, g=g, tc=tc: e.dma_start(
                            out=V_s[g * 4 + tc, :, :], in_=vst[tc % 2][:, :]),
                        vsem, 16)
        grp_last[g] = CNT["te"]
        pump(7)

    mW = loadW(wbuf, wq_in, CNT["te"])
    grp_last = {}
    for g in range(4):
        ybsem = "Lyb0" if g % 2 == 0 else "Lyb1"

        def ldy(e, g=g, ybsem=ybsem):
            for c in range(8):
                dld(e, ybsem, ybuf[g % 2][:, c * 512:(c + 1) * 512],
                    y1To_s[c * 128:(c + 1) * 128, g * 512:(g + 1) * 512])
            return None
        mldy = step("sync", [("te", grp_last.get(g - 2, 0))], ldy, ybsem, 16 * 8)
        for db in range(8):
            proj_block([(ybsem, mldy), ("Lwb", mW)], wbuf, ybuf[g % 2], db,
                       lambda e, buf, g=g, db=db: e.dma_start(
                           out=QT_s[db * 128:(db + 1) * 128, g * 512:(g + 1) * 512],
                           in_=buf[:, :]))
        grp_last[g] = CNT["te"]
        pump(8)
    pump(1000)

    def mred(e):
        e.reduce_max(ss[0:2, 100:101], ss[0:2, 0:64], axis=AXX)
        e.reduce_max(ss[0:2, 101:102], ss[0:2, 64:96], axis=AXX)
        return e.drain()
    mk_nr = step("vector", [("ve", CNT["ve"])], mred, "ve")
    mk_tp = step("tensor", [("ve", mk_nr)],
                 lambda e: e.matmul(psC[0:2, 0:2], ss[0:2, 100:102],
                                    ident[0:2, 0:2], is_transpose=True,
                                    start=True, stop=True), "te")

    def mprod(e):
        e.tensor_copy(sm[:, 10:12], psC[0:1, 0:2])
        e.drain()
        e.tensor_tensor(sm[:, 12:13], sm[0:1, 10:11], sm[0:1, 11:12],
                        AluOpType.mult)
        return e.drain()
    mk_pr = step("vector", [("te", mk_tp)], mprod, "ve")
    def msqf(e):
        e.activation(sm[:, 13:14], sm[0:1, 12:13], AF.Sqrt, scale=1.0 / 64)
        return e.drain()
    mk_sq = step("scalar", [("ve", mk_pr)], msqf, "se")

    def mneg(e):
        e.tensor_scalar(sm[:, 14:15], sm[0:1, 13:14], 1.01, 1e-6,
                        AluOpType.mult, AluOpType.add)
        e.drain()
        e.tensor_scalar(sm[:, 15:16], sm[0:1, 14:15], -1.0, None,
                        AluOpType.mult)
        return e.drain()
    mk_ng = step("vector", [("se", mk_sq)], mneg, "ve")
    mk_bc = step("tensor", [("ve", mk_ng)],
                 lambda e: e.matmul(psC[:, 0:1], onesrow[0:1, :], sm[0:1, 15:16],
                                    start=True, stop=True), "te")
    def ngf(e):
        e.tensor_copy(negmb[:, :], psC[:, 0:1])
        return e.drain()
    step("vector", [("te", mk_bc)], ngf, "ve")
    barrier()
    flush(block)

    qtb = arf[:, 0:4096]
    ktb = [arf[:, 4096:6144], arf[:, 6144:8192]]
    vbb = [arf[:, 8192:10272], arf[:, 10272:12352]]
    Pb = [arf[:, 12352:13376], arf[:, 13376:14400]]
    srp = [arf[:, 14400:14912], arf[:, 14912:15424]]
    ACC = 15424
    accs = [arf[:, ACC + h * 512:ACC + (h + 1) * 512] for h in range(16)]
    xob = [arf[:, 23616:24640], arf[:, 24640:25664]]
    xmst = [arf[:, 25664:26688], arf[:, 26688:27712]]
    wob = arf[:, 0:8192]
    attnT = arf[:, 8192:12288]

    o_done_prev = 0
    asm_done_prev = 0

    for q in range(NQ):
        def ldq(e, q=q):
            for c in range(8):
                dld(e, "Lqt", qtb[:, c * 512:(c + 1) * 512],
                    QT_s[c * 128:(c + 1) * 128, q * 512:(q + 1) * 512])
            return None
        m_qt = step("sync", [("te", o_done_prev)], ldq, "Lqt", 16 * 8)

        def zacc(e):
            for h in range(16):
                e.memset(accs[h][0:65, :], 0.0)
            return e.nop()
        m_z = step("vector", [("te", asm_done_prev)], zacc, "ve")

        exp_mk = {}
        pv_mk = {}
        msk_mk = {}
        add_mk = {}
        sc_mk = {}
        kg_ld = {}

        def prevmk(dd, kg, h, delta=2):
            if (kg, h - delta) in dd:
                return dd[(kg, h - delta)]
            return dd.get((kg - 1, h + 16 - delta), 0)

        for kg in range(KGQ[q]):
            kvsem = "Lkv0" if kg % 2 == 0 else "Lkv1"

            def ldkv(e, q=q, kg=kg, kvsem=kvsem):
                for c in range(8):
                    dld(e, kvsem, ktb[kg % 2][:, c * 256:(c + 1) * 256],
                        KT_s[c * 128:(c + 1) * 128, kg * 256:(kg + 1) * 256])
                for i in range(2):
                    dld(e, kvsem, vbb[kg % 2][:, i * 1040:(i + 1) * 1040],
                        V_s[kg * 2 + i, :, :])
                return None
            if kg >= 2:
                wl = [("te", pv_mk[(kg - 2, 15)])]
            else:
                wl = [("te", o_done_prev)]
            kg_ld[kg] = step("sync", wl, ldkv, kvsem, 16 * 10)

            def emit_scores(h):
                psS = psA if h % 2 == 0 else psA2
                kcs = [2 * kg, 2 * kg + 1]
                sfx = [_suffix(q, kc) for kc in kcs]

                def scores(e, kg=kg, h=h, kcs=kcs, sfx=sfx, psS=psS):
                    last = None
                    for i in range(2):
                        s = sfx[i]
                        last = e.matmul(
                            psS[:, i * 512 + s * 128:(i + 1) * 512],
                            ktb[kg % 2][(h % 2) * 64:(h % 2) * 64 + 64,
                                        (h // 2) * 256 + i * 128:(h // 2) * 256 + (i + 1) * 128],
                            qtb[(h % 2) * 64:(h % 2) * 64 + 64,
                                (h // 2) * 512 + s * 128:(h // 2) * 512 + 512],
                            start=True, stop=True)
                    return last
                kvsem = "Lkv0" if kg % 2 == 0 else "Lkv1"
                wl = [(kvsem, kg_ld[kg]), ("Lqt", m_qt),
                      ("se", prevmk(exp_mk, kg, h))]
                sc_mk[(kg, h)] = step("tensor", wl, scores, "te")

                def expf(e, h=h, psS=psS):
                    e.activation(Pb[h % 2][:, :], psS[:, :], AF.Exp,
                                 bias=negmb[:, :], scale=0.125)
                    return e.drain()
                wl = [("te", sc_mk[(kg, h)]), ("te", prevmk(pv_mk, kg, h))]
                exp_mk[(kg, h)] = step("scalar", wl, expf, "se")

                mts = []
                for i, kc in enumerate(kcs):
                    for r in range(sfx[i], 4):
                        if (q, kc, r) in MIDX:
                            mts.append((i, r, MIDX[(q, kc, r)]))
                if mts:
                    def maskf(e, h=h, mts=mts):
                        last = None
                        for i, r, idx in mts:
                            sl = Pb[h % 2][:, i * 512 + r * 128:i * 512 + (r + 1) * 128]
                            last = e.tensor_tensor(
                                sl, sl, maskb[:, idx * 128:(idx + 1) * 128],
                                AluOpType.mult)
                        return e.drain()
                    msk_mk[(kg, h)] = step("vector", [("se", exp_mk[(kg, h)])],
                                           maskf, "ve")

            def emit_pv(h):
                kcs = [2 * kg, 2 * kg + 1]
                sfx = [_suffix(q, kc) for kc in kcs]

                def pvf(e, kg=kg, h=h, sfx=sfx):
                    last = None
                    for i in range(2):
                        s = sfx[i]
                        last = e.matmul(
                            psB[0:65, (h % 2) * 512 + s * 128:(h % 2) * 512 + 512],
                            vbb[kg % 2][:, i * 1040 + h * 65:i * 1040 + h * 65 + 65],
                            Pb[h % 2][:, i * 512 + s * 128:(i + 1) * 512],
                            start=(i == 0), stop=(i == 1))
                    return last
                wl = [("se", exp_mk[(kg, h)]), ("ve", prevmk(add_mk, kg, h)),
                      ("ve", m_z)]
                if (kg, h) in msk_mk:
                    wl.append(("ve", msk_mk[(kg, h)]))
                pv_mk[(kg, h)] = step("tensor", wl, pvf, "te")

                sg = sfx[0]

                def addf(e, kg=kg, h=h, sg=sg):
                    sl = accs[h][0:65, sg * 128:512]
                    return e.tensor_tensor(
                        sl, sl,
                        psB[0:65, (h % 2) * 512 + sg * 128:(h % 2) * 512 + 512],
                        AluOpType.add)
                add_mk[(kg, h)] = step("vector", [("te", pv_mk[(kg, h)])], addf,
                                       "ve")

            for h in range(17):
                if h < 16:
                    emit_scores(h)
                if h >= 1:
                    emit_pv(h - 1)

        if "p3kg" in SER:
            barrier()
        last_kg = KGQ[q] - 1

        def ldwo(e):
            for c in range(8):
                dld(e, "Lwo", wob[:, c * 1024:(c + 1) * 1024],
                    wo_in[c * 128:(c + 1) * 128, :])
            return None
        m_wo = step("sync", [("te", pv_mk[(last_kg, 15)])], ldwo, "Lwo", 16 * 8)

        add_last = add_mk[(last_kg, 15)]
        normPs = [psB[:, 0:512], psB[:, 512:1024], psA[:, 0:512]]
        asmPs = [psC[:, 0:512], psC[:, 512:1024], psA2[:, 0:512]]
        asm_mk = {}
        bc_mk = {}
        rcp_mk = {}
        at_mk = {}
        for j in range(8):
            def rcpf(e, j=j):
                e.reciprocal(arf[0:1, 12352 + (j % 3) * 1024:
                                  12352 + (j % 3) * 1024 + 1024],
                             arf[64:65, ACC + 2 * j * 512:
                                 ACC + 2 * j * 512 + 1024])
                return e.drain()
            rcp_mk[j] = step("vector", [("ve", add_last),
                                        ("te", bc_mk.get(j - 3, 0))], rcpf, "ve")

            def bcf(e, j=j):
                base = 12352 + (j % 3) * 1024
                ps = normPs[j % 3]
                last = None
                for i in range(2):
                    last = e.matmul(
                        ps[i * 64:(i + 1) * 64, :],
                        onesrow[0:1, 0:64],
                        arf[0:1, base + i * 512:base + (i + 1) * 512],
                        start=True, stop=True)
                return last
            bc_mk[j] = step("tensor",
                            [("ve", rcp_mk[j]), ("ve", at_mk.get(j - 3, 0))],
                            bcf, "te")

            def asmf(e, j=j):
                ps = asmPs[j % 3]
                e.matmul(ps, ident[0:64, :], accs[2 * j][0:64, :],
                         start=True, stop=False)
                return e.matmul(ps, shiftm[:, :], accs[2 * j + 1][0:64, :],
                                start=False, stop=True)
            asm_mk[j] = step("tensor", [("ve", add_last)], asmf, "te")

            def atf1(e, j=j):
                e.copy(attnT[:, j * 512:(j + 1) * 512], asmPs[j % 3])
                return e.drain()
            at_se = step("scalar", [("te", asm_mk[j])], atf1, "se")

            def atf2(e, j=j):
                sl = attnT[:, j * 512:(j + 1) * 512]
                e.tensor_tensor(sl, sl, normPs[j % 3], AluOpType.mult)
                return e.drain()
            at_mk[j] = step("vector", [("se", at_se), ("te", bc_mk[j])],
                            atf2, "ve")

        xev_mk = {}
        o_mk = {}
        xst_mk = {}
        for tsub in range(4):
            xosem = "Lxo0" if tsub % 2 == 0 else "Lxo1"
            xo_mk = step("scalar", [("ve", xev_mk.get((tsub - 2, 1), 0))],
                         lambda e, q=q, tsub=tsub: e.dma_start(
                             out=xob[tsub % 2][:, :],
                             in_=x_own[(q * 4 + tsub) * 128:(q * 4 + tsub + 1) * 128, :]),
                         xosem, 16)
            for osl in range(2):
                pidx = (tsub * 2 + osl) % 2

                def omms(e, tsub=tsub, osl=osl, pidx=pidx):
                    last = None
                    for j in range(8):
                        last = e.matmul(
                            psC[:, pidx * 512:pidx * 512 + 512],
                            attnT[:, j * 512 + tsub * 128:j * 512 + tsub * 128 + 128],
                            wob[:, j * 1024 + osl * 512:j * 1024 + (osl + 1) * 512],
                            start=(j == 0), stop=(j == 7))
                    return last
                wl = [("ve", at_mk[7]), ("Lwo", m_wo),
                      ("ve", xev_mk.get((tsub - 1, osl), 0))]
                o_mk[(tsub, osl)] = step("tensor", wl, omms, "te")

                def xevf(e, tsub=tsub, osl=osl, pidx=pidx):
                    e.tensor_tensor(
                        xmst[tsub % 2][:, osl * 512:(osl + 1) * 512],
                        psC[:, pidx * 512:pidx * 512 + 512],
                        xob[tsub % 2][:, osl * 512:(osl + 1) * 512],
                        AluOpType.add)
                    return e.drain()
                xmsem = "Sxm0" if tsub % 2 == 0 else "Sxm1"
                wl = [("te", o_mk[(tsub, osl)]), (xosem, xo_mk),
                      (xmsem, xst_mk.get(tsub - 2, 0))]
                xev_mk[(tsub, osl)] = step("vector", wl, xevf, "ve")
            xmsem = "Sxm0" if tsub % 2 == 0 else "Sxm1"
            xst_mk[tsub] = step("gpsimd",
                                [("ve", xev_mk[(tsub, 0)]),
                                 ("ve", xev_mk[(tsub, 1)])],
                                lambda e, q=q, tsub=tsub: e.dma_start(
                                    out=xmid_s[(q * 4 + tsub) * 128:(q * 4 + tsub + 1) * 128, :],
                                    in_=xmst[tsub % 2][:, :]), xmsem, 16)
        if "p3tail" in SER:
            barrier()
        o_done_prev = o_mk[(3, 1)]
        asm_done_prev = asm_mk[7]
        flush(block)

    barrier()

    gh = [arf[:, i * 4096:(i + 1) * 4096] for i in range(4)]
    dumb = arf[:, 22528:23552]
    corr1 = arf[:, 23552:24576]
    corr2 = arf[:, 24576:25600]
    y2b = arf[:, 16384:17408]
    xmld = [arf[:, 17408:18432], arf[:, 18432:19456]]
    xout = arf[:, 19456:20480]
    xmr = [arf[:, 20480:21504], arf[:, 21504:22528]]
    y2qT = arb[:, 0:4096]
    wgb = [arb[:, 4096:8192], arb[:, 8192:12288]]
    wub = [arb[:, 12288:16384], arb[:, 16384:20480]]
    hqT = arb[:, 4096:20480]
    wdb = [arb[:, 20480:24576], arb[:, 24576:28672]]
    hqs = arb[:, 28672:29184]
    y2qb = arb[:, 29184:30208]

    if DEBUG2:
        def dbgdump2(e):
            e.dma_start(out=dbg_xmid2, in_=xmid_s[:, :]).then_inc(SEM["st"], 16)
            e.dma_start(out=dbg_wgq, in_=wgq_s[:, :]).then_inc(SEM["st"], 16)
            e.dma_start(out=dbg_wuq, in_=wuq_s[:, :]).then_inc(SEM["st"], 16)
            e.dma_start(out=dbg_wdq, in_=wdq_s[:, :]).then_inc(SEM["st"], 16)
            return None
        step("gpsimd", [], dbgdump2, "st", 64)
    if DEBUG:
        def dbgdump(e):
            e.dma_start(out=dbg_stat[:, 0:56], in_=stat[:, :]).then_inc(SEM["st"], 16)
            e.dma_start(out=dbg_stat[:, 56:57], in_=negmb[:, :], allow_slow_non_contiguous=True).then_inc(SEM["st"], 16)
            e.dma_start(out=dbg_stat[:, 57:60], in_=mw127[:, :], allow_slow_non_contiguous=True).then_inc(SEM["st"], 16)
            e.dma_start(out=dbg_stat[0:2, 60:62], in_=ss[:, 100:102], allow_slow_non_contiguous=True).then_inc(SEM["st"], 16)
            e.dma_start(out=dbg_stat[0:1, 62:63], in_=sm[:, 15:16], allow_slow_non_contiguous=True).then_inc(SEM["st"], 16)
            e.dma_start(out=dbg_xmid, in_=xmid_s[:, :]).then_inc(SEM["st"], 16)
            e.dma_start(out=dbg_y1T, in_=y1T_s[:, :]).then_inc(SEM["st"], 16)
            e.dma_start(out=dbg_KT, in_=KT_s[:, 0:512]).then_inc(SEM["st"], 16)
            e.dma_start(out=dbg_QT, in_=QT_s[:, 0:512]).then_inc(SEM["st"], 16)
            e.dma_start(out=dbg_P, in_=Pb[0][:, :]).then_inc(SEM["st"], 16)
            return None
        step("gpsimd", [], dbgdump, "st", 160)
    dn_ev_prev = 0
    dn_st_prev = 0
    dn23_prev = 0
    ln_prev = {}
    for tt in range(4):
        dprev = dn_ev_prev
        sprev = dn_st_prev
        y2tp_mk = {}
        ln_mk = {}
        tp_mk = {}
        for sub in range(4):
            sb = ST_LN + (sub % 2) * 13
            xlsem = "Lxl0" if sub % 2 == 0 else "Lxl1"
            mld = step("scalar", [("ve", ln_mk.get(sub - 2, ln_prev.get(sub + 2, 0)))],
                       lambda e, tt=tt, sub=sub: e.dma_start(
                           out=xmld[sub % 2][:, :],
                           in_=xmid_s[(tt * 4 + sub) * 128:(tt * 4 + sub + 1) * 128, :]),
                       xlsem, 16)

            def acts(e, sub=sub, sb=sb):
                e.activation(dumb[:, :], xmld[sub % 2][:, :], AF.Square,
                             accum_out=stat[:, sb:sb + 1])
                e.activation(dumb[:, :], xmld[sub % 2][:, :], AF.Copy,
                             accum_out=stat[:, sb + 1:sb + 2])
                return e.drain()
            macc = step("scalar", [(xlsem, mld),
                                   ("ve", ln_mk.get(sub - 2, ln_prev.get(sub + 2, 0)))],
                        acts, "se")

            def stats(e, sb=sb):
                e.tensor_scalar(stat[:, sb + 2:sb + 3], stat[:, sb + 1:sb + 2],
                                1.0 / 1024, None, AluOpType.mult)
                e.drain()
                e.tensor_tensor(stat[:, sb + 4:sb + 5], stat[:, sb + 2:sb + 3],
                                stat[:, sb + 2:sb + 3], AluOpType.mult)
                e.drain()
                e.scalar_tensor_tensor(stat[:, sb + 6:sb + 7],
                                       stat[:, sb:sb + 1], 1.0 / 1024,
                                       stat[:, sb + 4:sb + 5],
                                       AluOpType.mult, AluOpType.subtract)
                return e.drain()
            mvar = step("vector", [("se", macc)], stats, "ve")
            def msqf2(e, sb=sb):
                e.activation(stat[:, sb + 7:sb + 8], stat[:, sb + 6:sb + 7],
                             AF.Sqrt, bias=epsb[:, :])
                return e.drain()
            msq = step("scalar", [("ve", mvar)], msqf2, "se")

            def nq(e, sub=sub, sb=sb):
                e.reciprocal(stat[:, sb + 8:sb + 9], stat[:, sb + 7:sb + 8])
                e.drain()
                e.tensor_scalar(y2b[:, :], xmld[sub % 2][:, :],
                                stat[:, sb + 2:sb + 3], stat[:, sb + 8:sb + 9],
                                AluOpType.subtract, AluOpType.mult)
                e.reduce_max(stat[:, ST_YQ:ST_YQ + 1], y2b[:, :], axis=AXX,
                             apply_absolute_value=True)
                e.drain()
                e.tensor_scalar(stat[:, ST_YQ + 1:ST_YQ + 2],
                                stat[:, ST_YQ:ST_YQ + 1], 1e-5, None, AluOpType.max)
                e.drain()
                e.reciprocal(stat[:, ST_YQ + 2:ST_YQ + 3],
                             stat[:, ST_YQ + 1:ST_YQ + 2])
                e.drain()
                e.tensor_scalar(stat[:, ST_YQ + 3:ST_YQ + 4],
                                stat[:, ST_YQ + 2:ST_YQ + 3], 127.0, None,
                                AluOpType.mult)
                e.tensor_scalar(stat[:, ST_CTG + sub:ST_CTG + sub + 1],
                                stat[:, ST_YQ + 1:ST_YQ + 2], mw127[:, 0:1], None,
                                AluOpType.mult)
                e.tensor_scalar(stat[:, ST_CTU + sub:ST_CTU + sub + 1],
                                stat[:, ST_YQ + 1:ST_YQ + 2], mw127[:, 1:2], None,
                                AluOpType.mult)
                e.drain()
                e.tensor_scalar(y2b[:, :], y2b[:, :], stat[:, ST_YQ + 3:ST_YQ + 4],
                                None, AluOpType.mult)
                e.tensor_copy(i32y[:, :], y2b[:, :])
                e.tensor_copy(corr1[:, :], i32y[:, :])
                e.tensor_tensor(y2b[:, :], y2b[:, :], corr1[:, :], AluOpType.subtract)
                e.tensor_scalar(corr2[:, :], y2b[:, :], 0.5, None, AluOpType.is_gt)
                e.tensor_scalar(y2b[:, :], y2b[:, :], -0.5, None, AluOpType.is_lt)
                e.tensor_tensor(corr2[:, :], corr2[:, :], y2b[:, :],
                                AluOpType.subtract)
                e.tensor_tensor(y2b[:, :], corr1[:, :], corr2[:, :],
                                AluOpType.add)
                return e.drain()
            mq_ = step("vector", [("se", msq), ("te", tp_mk.get(sub - 1, 0))], nq,
                       "ve")
            ln_mk[sub] = mq_

            psT = psB if sub % 2 == 0 else psC

            def tps(e, psT=psT):
                last = None
                for c in range(8):
                    last = e.matmul(psT[:, c * 128:(c + 1) * 128],
                                    y2b[:, c * 128:(c + 1) * 128], ident[:, :],
                                    is_transpose=True, start=True, stop=True)
                return last
            mtp = step("tensor", [("ve", mq_), ("ve", y2tp_mk.get(sub - 2, dprev))],
                       tps, "te")
            tp_mk[sub] = mtp
            def y2tpf(e, sub=sub, psT=psT):
                e.tensor_copy(y2qT[:, sub * 1024:(sub + 1) * 1024], psT[:, :])
                return e.drain()
            y2tp_mk[sub] = step("vector", [("te", mtp)], y2tpf, "ve")
            if DEBUG and tt == 0 and sub == 3:
                step("gpsimd", [("ve", y2tp_mk[sub])],
                     lambda e: e.dma_start(out=dbg_y2qT, in_=y2qT[:, 0:4096]), "st", 16)

        if "p4ln" in SER:
            barrier()
        gu_te = {}
        gu_se = {}
        gu_ve = {}
        psGU = [psA[:, 0:512], psA[:, 512:1024], psA2[:, 0:512], psA2[:, 512:1024]]
        gidx = 0
        for isl in range(8):
            wgsem = "Lwg0" if isl % 2 == 0 else "Lwg1"

            def ldwgu(e, isl=isl, wgsem=wgsem):
                for c in range(8):
                    dld(e, wgsem, wgb[isl % 2][:, c * 512:(c + 1) * 512],
                        wgq_s[c * 128:(c + 1) * 128, isl * 512:(isl + 1) * 512])
                    dld(e, wgsem, wub[isl % 2][:, c * 512:(c + 1) * 512],
                        wuq_s[c * 128:(c + 1) * 128, isl * 512:(isl + 1) * 512])
                return None
            wld = step("sync", [("te", gu_te.get(isl - 2, dn23_prev))], ldwgu,
                       wgsem, 16 * 16)
            for sub in range(4):
                pg = psGU[gidx % 4]
                wl = [(wgsem, wld), ("ve", y2tp_mk[sub]),
                      ("se", gu_se.get(gidx - 4, 0)), ("ve", gu_ve.get(gidx - 4, 0))]
                if gidx < 4:
                    wl.append(("ve", dprev))

                def gmms(e, isl=isl, sub=sub, pg=pg):
                    last = None
                    for c in range(8):
                        last = e.matmul(
                            pg,
                            y2qT[:, sub * 1024 + c * 128:sub * 1024 + (c + 1) * 128],
                            wgb[isl % 2][:, c * 512:(c + 1) * 512],
                            start=(c == 0), stop=(c == 7))
                    return last
                mg = step("tensor", wl, gmms, "te")
                def siluf(e, isl=isl, sub=sub, pg=pg):
                    e.activation(gh[sub][:, isl * 512:(isl + 1) * 512], pg, AF.Silu,
                                 scale=stat[:, ST_CTG + sub:ST_CTG + sub + 1])
                    return e.drain()
                mge = step("scalar", [("te", mg)], siluf, "se")
                gu_se[gidx] = mge
                pu = psGU[(gidx + 1) % 4]
                wl = [("se", gu_se.get(gidx - 3, 0)), ("ve", gu_ve.get(gidx - 3, 0))]
                if gidx < 3:
                    wl.append(("ve", dprev))

                def umms(e, isl=isl, sub=sub, pu=pu):
                    last = None
                    for c in range(8):
                        last = e.matmul(
                            pu,
                            y2qT[:, sub * 1024 + c * 128:sub * 1024 + (c + 1) * 128],
                            wub[isl % 2][:, c * 512:(c + 1) * 512],
                            start=(c == 0), stop=(c == 7))
                    return last
                mu_ = step("tensor", wl, umms, "te")
                gu_te[isl] = mu_

                def ustt(e, isl=isl, sub=sub, pu=pu):
                    sl = gh[sub][:, isl * 512:(isl + 1) * 512]
                    return e.scalar_tensor_tensor(
                        sl, pu, stat[:, ST_CTU + sub:ST_CTU + sub + 1], sl,
                        AluOpType.mult, AluOpType.mult)
                gu_ve[gidx] = step("vector", [("te", mu_), ("se", mge)], ustt, "ve")
                gidx += 2
        if "p4gu" in SER:
            barrier()
        all_gu_te = CNT["te"]
        all_gu_se = CNT["se"]
        all_gu_ve = CNT["ve"]
        dbg_gh_st = 0
        if DEBUG and tt == 0:
            dbg_gh_st = step("gpsimd", [("ve", all_gu_ve), ("se", all_gu_se)],
                             lambda e: e.dma_start(out=dbg_gh, in_=gh[0][:, :]),
                             "st", 16)

        msc_mk = {}
        for sub in range(4):
            def hscale(e, sub=sub):
                e.reduce_max(stat[:, 51:52], gh[sub][:, :], axis=AXX,
                             apply_absolute_value=True)
                e.drain()
                e.tensor_scalar(stat[:, 52:53], stat[:, 51:52], 1e-5, None,
                                AluOpType.max)
                e.drain()
                e.reciprocal(stat[:, 53:54], stat[:, 52:53])
                e.drain()
                e.tensor_scalar(stat[:, ST_HQ + sub:ST_HQ + sub + 1],
                                stat[:, 53:54], 127.0, None, AluOpType.mult)
                e.tensor_scalar(stat[:, ST_C2 + sub:ST_C2 + sub + 1],
                                stat[:, 52:53], mw127[:, 2:3],
                                None, AluOpType.mult)
                return e.drain()
            msc_mk[sub] = step("vector", [("ve", all_gu_ve)], hscale, "ve")

        fbufs = [arf[:, 25600:27648], arf[:, 27648:29696]]
        hw_mk = {}
        cast_mk = {}
        tp_mk2 = {}
        ev_mk = {}
        HQI = [(sub, half) for sub in range(4) for half in range(2)]

        def emit_cast(idx):
            sub, half = HQI[idx]

            def castf(e, sub=sub, half=half, idx=idx):
                sl = gh[sub][:, half * 2048:(half + 1) * 2048]
                e.activation(i32t[:, :], sl, AF.Copy,
                             scale=stat[:, ST_HQ + sub:ST_HQ + sub + 1])
                e.drain()
                e.activation(fbufs[idx % 2][:, :], i32t[:, :], AF.Copy)
                return e.drain()
            cwl = [("ve", msc_mk[sub]), ("ve", hw_mk.get(idx - 2, 0))]
            if idx == 0:
                cwl.append(("st", dbg_gh_st))
            cast_mk[idx] = step("scalar", cwl, castf, "se")

        emit_cast(0)
        hcnt = 0
        for idx, (sub, half) in enumerate(HQI):
            if idx + 1 < 8:
                emit_cast(idx + 1)

            def chainf(e, sub=sub, half=half, idx=idx):
                sl = gh[sub][:, half * 2048:(half + 1) * 2048]
                fb = fbufs[idx % 2][:, :]
                e.scalar_tensor_tensor(sl, sl,
                                       stat[:, ST_HQ + sub:ST_HQ + sub + 1],
                                       fb, AluOpType.mult, AluOpType.subtract)
                e.scalar_tensor_tensor(fb, sl, 0.5, fb,
                                       AluOpType.is_gt, AluOpType.add)
                e.tensor_scalar(sl, sl, -0.5, None, AluOpType.is_lt)
                e.tensor_tensor(sl, fb, sl, AluOpType.subtract)
                return e.drain()
            hw_mk[idx] = step("vector", [("se", cast_mk[idx])], chainf, "ve")
            for cc in range(4):
                    psT = psB[:, (hcnt % 2) * 512:(hcnt % 2) * 512 + 512]
                    isl = half * 4 + cc

                    def hq2(e, sub=sub, isl=isl, psT=psT):
                        last = None
                        for c in range(4):
                            last = e.matmul(
                                psT[:, c * 128:(c + 1) * 128],
                                gh[sub][:, isl * 512 + c * 128:isl * 512 + (c + 1) * 128],
                                ident[:, :],
                                is_transpose=True, start=True, stop=True)
                        return last
                    tp_mk2[hcnt] = step("tensor",
                                        [("ve", hw_mk[idx]),
                                         ("se", ev_mk.get(hcnt - 2, 0)),
                                         ("ve", dprev if hcnt == 0 else 0)],
                                        hq2, "te")

                    def hq3(e, sub=sub, isl=isl, psT=psT):
                        off = sub * 4096 + isl * 512
                        e.activation(hqT[:, off:off + 512], psT, AF.Copy)
                        return e.drain()
                    ev_mk[hcnt] = step("scalar", [("te", tp_mk2[hcnt])], hq3,
                                       "se")
                    hcnt += 1

        all_tp_ve = CNT["ve"]
        all_tp_se = CNT["se"]
        psDN = [[psA[:, 0:512], psA[:, 512:1024]],
                [psA2[:, 0:512], psA2[:, 512:1024]],
                [psC[:, 0:512], psC[:, 512:1024]],
                [psB[:, 0:512], psB[:, 512:1024]]]
        dn23_te = {}
        dwld2 = {}
        for icg in range(8):
            wdsem = "Lwd0" if icg % 2 == 0 else "Lwd1"

            def lddw2(e, icg=icg, wdsem=wdsem):
                for cc in range(4):
                    dld(e, wdsem, wdb[icg % 2][:, cc * 1024:(cc + 1) * 1024],
                        wdq_s[icg * 512 + cc * 128:icg * 512 + (cc + 1) * 128, :])
                return None
            wl = [("te", dn23_te.get(icg - 2, dn23_prev if icg < 2 else 0))]
            dwld2[icg] = step("sync", wl, lddw2, wdsem, 16 * 4)
            wl = [(wdsem, dwld2[icg]), ("ve", all_tp_ve), ("se", all_tp_se)]

            def dmms23(e, icg=icg):
                last = None
                for sub in range(4):
                    for ic4 in range(4):
                        for osl in range(2):
                            last = e.matmul(
                                psDN[sub][osl],
                                hqT[:, sub * 4096 + (icg * 4 + ic4) * 128:
                                    sub * 4096 + (icg * 4 + ic4) * 128 + 128],
                                wdb[icg % 2][:, ic4 * 1024 + osl * 512:
                                             ic4 * 1024 + (osl + 1) * 512],
                                start=(icg == 0 and ic4 == 0),
                                stop=(icg == 7 and ic4 == 3))
                return last
            dn23_te[icg] = step("tensor", wl, dmms23, "te")

        dn_ev = {}
        dn_st = {}
        for sub in range(4):
            xrsem = "Lxr0" if sub % 2 == 0 else "Lxr1"
            mxr = step("scalar", [("ve", dn_ev.get(sub - 2, dprev))],
                       lambda e, tt=tt, sub=sub: e.dma_start(
                           out=xmr[sub % 2][:, :],
                           in_=xmid_s[(tt * 4 + sub) * 128:(tt * 4 + sub + 1) * 128, :]),
                       xrsem, 16)

            def dnev(e, sub=sub):
                for osl in range(2):
                    e.scalar_tensor_tensor(
                        xout[:, osl * 512:(osl + 1) * 512],
                        psDN[sub][osl],
                        stat[:, ST_C2 + sub:ST_C2 + sub + 1],
                        xmr[sub % 2][:, osl * 512:(osl + 1) * 512],
                        AluOpType.mult, AluOpType.add)
                return e.drain()
            wl = [("te", dn23_te[7]), (xrsem, mxr), ("Sy", dn_st.get(sub - 1, sprev))]
            dn_ev[sub] = step("vector", wl, dnev, "ve")
            dn_st[sub] = step("gpsimd", [("ve", dn_ev[sub])],
                              lambda e, tt=tt, sub=sub: e.dma_start(
                                  out=y_own[(tt * 4 + sub) * 128:(tt * 4 + sub + 1) * 128, :],
                                  in_=xout[:, :]), "Sy", 16)
        if "p4dn" in SER:
            barrier()
        dn_ev_prev = dn_ev[3]
        dn_st_prev = dn_st[3]
        dn23_prev = dn23_te[7]
        ln_prev = ln_mk
        flush(block)

    step("gpsimd", [(n, CNT[n]) for n in SSEMS], lambda e: e.nop(), "ge")
    flush(block)
    blk.__exit__(None, None, None)
    return nc


_NC = None


def kernel(**inputs):
    global _NC
    x = np.asarray(inputs["L0_attn_x"], dtype=np.float32)
    wq = np.ascontiguousarray(np.asarray(inputs["L0_attn_W_Q"], np.float32))
    wk = np.ascontiguousarray(np.asarray(inputs["L0_attn_W_K"], np.float32))
    wv = np.ascontiguousarray(np.asarray(inputs["L0_attn_W_V"], np.float32))
    wo = np.ascontiguousarray(np.asarray(inputs["L0_attn_W_o"], np.float32))
    wgT = np.ascontiguousarray(np.asarray(inputs["L0_ffn_w_gate"], np.float32).T)
    wuT = np.ascontiguousarray(np.asarray(inputs["L0_ffn_w_up"], np.float32).T)
    wdT = np.ascontiguousarray(np.asarray(inputs["L0_ffn_w_down"], np.float32).T)

    blocks = (B0, B1)
    dl = {}
    for par in range(2):
        d = np.zeros((128, NM), np.float32)
        for i, (q, kc, r) in enumerate(MASKS):
            g = blocks[par][4 * q + r]
            d[:, i] = 128.0 * (kc - g)
        dl[par] = d

    in_maps = []
    for c in range(8):
        seq, par = c // 2, c % 2
        rows = np.concatenate([np.arange(b * 128, (b + 1) * 128)
                               for b in blocks[par]])
        in_maps.append({
            "x_full": np.ascontiguousarray(x[seq]),
            "x_own": np.ascontiguousarray(x[seq][rows]),
            "deltas": dl[par],
            "wq_in": wq, "wk_in": wk, "wv_in": wv, "wo_in": wo,
            "wgT_in": wgT, "wuT_in": wuT, "wdT_in": wdT,
        })

    if _NC is None:
        _NC = build_nc()
    res = run_bass_kernel_spmd(_NC, in_maps, core_ids=list(range(8)))

    y = np.empty((4, S, D), np.float32)
    for c in range(8):
        seq, par = c // 2, c % 2
        out = np.asarray(res.results[c]["y_own"])
        for r, b in enumerate(blocks[par]):
            y[seq, b * 128:(b + 1) * 128, :] = out[r * 128:(r + 1) * 128, :]
    return y


if __name__ == "__main__":
    import time
    t0 = time.time()
    nc = build_nc()
    print(f"build ok in {time.time() - t0:.1f}s")
